# version 22
# baseline (speedup 1.0000x reference)
import contextlib
import numpy as np
import concourse.bass as bass
import concourse.mybir as mybir
from concourse.bass_utils import run_bass_kernel_spmd

F32 = mybir.dt.float32
A = mybir.AluOpType
AF = mybir.ActivationFunctionType
AX = mybir.AxisListType

D = 1024
NH = 16
DK = 64
II = 2816
NCORE = 8
KT = 8
MT = 8
IT = 22
VTW = NH * 65

SY, TE, VE, SC, GP = "sync", "tensor", "vector", "scalar", "gpsimd"


class Prog:

    ENGINES = (SY, TE, VE, SC, GP)

    def __init__(self):
        self.ops = {e: [] for e in self.ENGINES}
        self.cnt = {e: 0 for e in self.ENGINES}
        self.state = {}
        self.regions = {}

    def _rdeps(self, t, r):
        deps = []
        if r is None:
            for reg in self.regions.get(t, ()):
                w = self.state.get((t, reg), {}).get("w")
                if w:
                    deps.append(w)
        else:
            for reg in (r, None):
                w = self.state.get((t, reg), {}).get("w")
                if w:
                    deps.append(w)
        return deps

    def _wdeps(self, t, r):
        deps = self._rdeps(t, r)
        if r is None:
            for reg in self.regions.get(t, ()):
                deps.extend(self.state.get((t, reg), {}).get("r", ()))
        else:
            for reg in (r, None):
                deps.extend(self.state.get((t, reg), {}).get("r", ()))
        return deps

    def op(self, eng, fn, reads=(), writes=(), dma=None):
        norm = lambda x: x if isinstance(x, tuple) else (x, None)
        reads = [norm(x) for x in reads]
        writes = [norm(x) for x in writes]
        deps = []
        for t, r in reads:
            deps.extend(self._rdeps(t, r))
        for t, r in writes:
            deps.extend(self._wdeps(t, r))
        waits = {}
        for s, v in deps:
            waits[s] = max(waits.get(s, 0), v)
        for s in list(waits):
            if s not in self.ENGINES:
                waits[s] = self.cnt.get(s, 0)
        sem = dma if dma else eng
        inc = 16 if dma else 1
        self.cnt[sem] = self.cnt.get(sem, 0) + inc
        my = (sem, self.cnt[sem])
        self.ops[eng].append((waits, fn, sem, inc))
        for t, r in reads:
            self.regions.setdefault(t, set()).add(r)
            self.state.setdefault((t, r), {"w": None, "r": []})["r"].append(my)
        for t, r in writes:
            self.regions.setdefault(t, set()).add(r)
            ent = self.state.setdefault((t, r), {"w": None, "r": []})
            ent["w"] = my
            ent["r"] = []
        return my

    def barrier(self):
        snap = dict(self.cnt)
        for e in self.ENGINES:
            waits = {s: v for s, v in snap.items() if v > 0}
            self.ops[e].append((waits, None, e, 1))
            self.cnt[e] += 1
        snap2 = {e: self.cnt[e] for e in self.ENGINES}
        for e in self.ENGINES:
            waits = {s: snap2[s] for s in self.ENGINES if s != e}
            self.ops[e].append((waits, None, e, 1))
            self.cnt[e] += 1
        self.state = {}
        self.regions = {}

    def final_wait(self, eng):
        waits = {s: v for s, v in self.cnt.items()
                 if s not in self.ENGINES and v > 0}
        self.ops[eng].append((waits, None, eng, 0))

    def emit(self, blk, sems):
        def runner(name):
            hwm = {}

            def body(eng):
                for waits, fn, sem, inc in self.ops[name]:
                    for s in sorted(waits):
                        v = waits[s]
                        if hwm.get(s, -1) >= v:
                            continue
                        eng.wait_ge(sems[s], v)
                        hwm[s] = v
                    if fn is None:
                        if inc:
                            eng.nop(nofuse=True).then_inc(sems[sem], inc)
                        continue
                    fn(eng).then_inc(sems[sem], inc)
            return body

        blk.sync(runner(SY))
        blk.tensor(runner(TE))
        blk.vector(runner(VE))
        blk.scalar(runner(SC))
        blk.gpsimd(runner(GP))


def bankA(c0, c1):
    return [("psA", b) for b in range(c0 // 512, (c1 + 511) // 512)]


def bankB(c0, c1):
    return [("psB", b) for b in range(c0 // 512, (c1 + 511) // 512)]


def build(S=2048):
    assert S % 512 == 0
    NCH1 = S // 512
    S2 = min(1024, S)
    NCH2 = S // S2
    NVT = S // 128
    NTQ = S // 512

    nc = bass.Bass("TRN2", target_bir_lowering=False)

    din = lambda n, sh: nc.dram_tensor(n, sh, F32, kind="ExternalInput").ap()
    dtmp = lambda n, sh: nc.dram_tensor(n, sh, F32, kind="Internal").ap()

    x_fm = din("x_fm", [D, S])
    zq_t = din("zq_t", [D, D])
    zk_t = din("zk_t", [D, D])
    wv_T = din("wv_T", [D, D])
    wg_T = din("wg_T", [D, II])
    wu_T = din("wu_T", [D, II])
    wd_T = din("wd_T", [II, D])
    bqk_pp = din("bqk_pp", [128, 16])
    bv_row = din("bv_row", [1, D])
    sel64 = din("sel64", [16, D])
    eye = din("eye", [128, 128])
    consts = din("consts", [128, 8])
    out_fm = nc.dram_tensor("out_fm", [D, S], F32, kind="ExternalOutput").ap()

    q_sp = dtmp("q_sp", [D, S])
    k_sp = dtmp("k_sp", [D, S])
    xq_sp = dtmp("xq_sp", [D, S])
    xn_sp = dtmp("xn_sp", [D, S])
    at_sp = dtmp("at_sp", [D, S])
    wqg_d = dtmp("wqg_d", [D, II])
    wqu_d = dtmp("wqu_d", [D, II])
    wqd_d = dtmp("wqd_d", [II, D])
    hq_sp = dtmp("hq_sp", [II, S])
    rs_dr = dtmp("rs_dr", [16, S])

    P = Prog()

    ap_of = {}

    def sbuf(name, shape):
        h = nc.alloc_sbuf_tensor(name, shape, F32)
        ap_of[name] = h.ap()
        return ap_of[name]

    b_x = sbuf("b_x", [128, 7040])
    b_xn = sbuf("b_xn", [128, 4096])
    b_qk = sbuf("b_qk", [128, 8192])
    b_vt = sbuf("b_vt", [128, max(NVT * VTW, 8 * S, IT * 512)])
    b_pw = sbuf("b_pw", [128, 2048])
    b_scr = sbuf("b_scr", [128, 3072])
    b_at = sbuf("b_at", [128, 8192])
    b_sm = sbuf("b_sm", [128, 2080])
    b_ct = sbuf("b_ct", [128, 1408])

    rem = nc.sbuf_bytes_remaining

    CONSTVALS = [1e-5, -0.5, 0.5, 2.0, -2.0, 4.0, 0.125, -1.0]
    for i, v in enumerate(CONSTVALS):
        nc.const_aps.register(F32, float(v), b_sm[:, 1024 + i:1024 + i + 1])

    ones_col = b_sm[:, 0:1]
    ones_row = b_sm[0:1, 2:130]
    c_bqk = b_sm[:, 130:146]
    pc = b_sm[:, 146:210]
    colsq = b_sm[:, 210:218]
    zn_s = b_sm[:, 218:226]
    e2_s = b_sm[:, 226:234]
    em_s = b_sm[:, 234:242]
    rzn_s = b_sm[:, 242:250]
    bv_rep = b_sm[:, 1040:2064]

    t_eye = sbuf("t_eye", [128, 128])

    psA = nc.alloc_psum_tensor("psA", [128, 2048], F32).ap()
    psB = nc.alloc_psum_tensor("psB", [128, 2048], F32).ap()

    P.op(VE, lambda e: e.memset(ones_col, 1.0), writes=["ones"])
    P.op(VE, lambda e: e.memset(ones_row, 1.0), writes=["ones"])
    P.op(SY, lambda e: e.dma_start(out=b_sm[:, 1024:1032], in_=consts[:]),
         writes=["consts"], dma="cc")
    P.op(SY, lambda e: e.dma_start(out=c_bqk, in_=bqk_pp[:]), writes=["bqk"], dma="cc")
    P.op(SY, lambda e: e.dma_start(out=t_eye[:], in_=eye[:]), writes=["eye"], dma="cc")
    bvr = b_xn[0:1, 2304:3328]
    P.op(SY, lambda e: e.dma_start(out=bvr, in_=bv_row[:]), writes=["bvr"], dma="cc")
    for c in range(2):
        P.op(TE, lambda e, c=c: e.matmul(psA[:, c * 512:(c + 1) * 512],
                                         ones_row[0:1, 0:128],
                                         bvr[0:1, c * 512:(c + 1) * 512],
                                         start=True, stop=True),
             reads=["ones", "bvr"], writes=[("psA", c)])
        P.op(VE, lambda e, c=c: e.tensor_copy(bv_rep[:, c * 512:(c + 1) * 512],
                                              psA[:, c * 512:(c + 1) * 512]),
             reads=[("psA", c)], writes=["bvrep"])

    def poin_consts(zt, bco, pco):
        for mt in range(MT):
            w = b_pw[:, (mt % 2) * 1024:(mt % 2) * 1024 + 1024]
            P.op(SY, lambda e, mt=mt, w=w, zt=zt: e.dma_start(
                out=w, in_=zt[mt * 128:(mt + 1) * 128, :]),
                writes=[("pw", mt % 2)], dma=f"zw{mt % 2}")
            for kt in range(KT):
                P.op(SC, lambda e, w=w, kt=kt: e.activation(
                    b_scr[:, 0:128], w[:, kt * 128:(kt + 1) * 128], AF.Square),
                    reads=[("pw", mt % 2)], writes=["sc0"])
                P.op(TE, lambda e, kt=kt: e.matmul(
                    psB[0:1, 0:128], ones_col, b_scr[:, 0:128],
                    start=(kt == 0), stop=(kt == KT - 1)),
                    reads=["sc0", "ones"], writes=[("psB", 0)])
            P.op(VE, lambda e: e.tensor_copy(b_scr[0:1, 512:640], psB[0:1, 0:128]),
                 reads=[("psB", 0)], writes=["sc1"])
            P.op(TE, lambda e, mt=mt: e.transpose(
                psA[:, 512 + mt:513 + mt], b_scr[0:1, 512:640],
                t_eye[0:1, 0:1]),
                reads=["sc1", "eye"], writes=[("psA", 1)])
        P.op(VE, lambda e: e.tensor_copy(colsq, psA[:, 512:520]),
             reads=[("psA", 1)], writes=["colsq"])
        P.op(SC, lambda e: e.activation(zn_s, colsq, AF.Ln),
             reads=["colsq"], writes=["zn"])
        P.op(SC, lambda e: e.activation(zn_s, zn_s, AF.Exp, scale=0.5),
             reads=["zn", "consts"], writes=["zn"])
        P.op(VE, lambda e: e.tensor_scalar(zn_s, zn_s, 1e-15, None, A.max),
             reads=["zn"], writes=["zn"])
        bc = c_bqk[:, bco:bco + 8]
        P.op(SC, lambda e, bc=bc: e.activation(e2_s, bc, AF.Exp, scale=2.0),
             reads=["bqk", "consts"], writes=["e2"])
        P.op(SC, lambda e, bc=bc: e.activation(em_s, bc, AF.Exp, scale=-2.0),
             reads=["bqk", "consts"], writes=["em"])
        c1 = pc[:, pco:pco + 8]
        s2r = pc[:, pco + 8:pco + 16]
        t2 = pc[:, pco + 16:pco + 24]
        nt2 = pc[:, pco + 24:pco + 32]
        P.op(VE, lambda e: e.reciprocal(rzn_s, zn_s), reads=["zn"], writes=["rzn"])
        P.op(VE, lambda e, s2r=s2r: e.tensor_sub(s2r, e2_s, em_s),
             reads=["e2", "em"], writes=["pcs"])
        P.op(VE, lambda e, s2r=s2r: e.tensor_scalar(s2r, s2r, 0.5, None, A.mult),
             reads=["pcs"], writes=["pcs"])
        P.op(VE, lambda e, c1=c1: e.tensor_add(c1, e2_s, em_s),
             reads=["e2", "em"], writes=["pcc"])
        P.op(VE, lambda e, c1=c1: e.tensor_scalar(c1, c1, 0.5, None, A.mult),
             reads=["pcc"], writes=["pcc"])
        P.op(VE, lambda e, c1=c1: e.tensor_mul(c1, c1, rzn_s),
             reads=["pcc", "rzn"], writes=["pcc"])
        P.op(VE, lambda e, t2=t2: e.tensor_scalar(t2, zn_s, 2.0, None, A.mult),
             reads=["zn"], writes=["pct"])
        P.op(VE, lambda e, nt2=nt2: e.tensor_scalar(nt2, zn_s, -2.0, None, A.mult),
             reads=["zn"], writes=["pcn"])

    poin_consts(zq_t, 0, 0)
    poin_consts(zk_t, 8, 32)

    P.barrier()

    nxin = b_scr[:, 0:512]
    nxt = b_scr[:, 512:1024]
    nxu = b_scr[:, 1024:1536]
    nxa = b_scr[:, 1536:2048]
    nxg = b_scr[:, 2048:2560]
    nxh = b_scr[:, 2560:3072]
    nxbm = b_ct[:, 0:32]
    nxsc = b_ct[:, 32:64]

    def emit_qtile(src, dst, r0, cs, W, SL=None, NM="nx_", ch="xn0", qeng=GP):
        nb = W // 16
        if SL is None:
            _in, _t, _u, _a, _g, _h = nxin, nxt, nxu, nxa, nxg, nxh
            _bm, _sc = nxbm, nxsc
        else:
            _in, _t, _u, _a, _g, _h, _bm, _sc = SL
        K_in, K_t, K_u, K_a, K_g, K_h, K_bm, K_sc = (
            NM + "in", NM + "t", NM + "u", NM + "a", NM + "g", NM + "h",
            NM + "bm", NM + "sc")
        P.op(qeng, lambda e: e.dma_start(
            out=_in[:, 0:W], in_=src[r0:r0 + 128, cs:cs + W]),
            writes=[K_in], dma=ch)
        P.op(VE, lambda e: e.transpose(_t[:, 0:W], _in[:, 0:W]),
             reads=[K_in], writes=[K_t])
        P.op(VE, lambda e: e.reduce_max(
            _bm[:, 0:nb], _t[:, 0:W].rearrange("p (b c) -> p b c", c=16),
            AX.X, apply_absolute_value=True),
            reads=[K_t], writes=[K_bm])
        P.op(VE, lambda e: e.tensor_scalar(_sc[:, 0:nb], _bm[:, 0:nb],
                                           1.0 / 6.0, 1e-12, A.mult, A.max),
             reads=[K_bm], writes=[K_sc])
        P.op(VE, lambda e: e.reciprocal(_bm[:, 0:nb], _sc[:, 0:nb]),
             reads=[K_sc], writes=[K_bm])
        scb = _sc[:, 0:nb].rearrange("p (b o) -> p b o", o=1) \
            .broadcast_to([128, nb, 16])
        rcb = _bm[:, 0:nb].rearrange("p (b o) -> p b o", o=1) \
            .broadcast_to([128, nb, 16])
        P.op(VE, lambda e: e.tensor_tensor(
            _u[:, 0:W].rearrange("p (b c) -> p b c", c=16),
            _t[:, 0:W].rearrange("p (b c) -> p b c", c=16), rcb, A.mult),
            reads=[K_t, K_bm], writes=[K_u])
        P.op(VE, lambda e: e.scalar_tensor_tensor(
            _a[:, 0:W], _u[:, 0:W], -1.0, _u[:, 0:W], A.mult, A.max),
            reads=[K_u], writes=[K_a])
        P.op(VE, lambda e: e.tensor_scalar(_g[:, 0:W], _a[:, 0:W],
                                           0.25, None, A.is_gt),
             reads=[K_a], writes=[K_g])
        for th in (0.75, 1.25, 1.75):
            P.op(VE, lambda e, th=th: e.scalar_tensor_tensor(
                _g[:, 0:W], _a[:, 0:W], th, _g[:, 0:W], A.is_gt, A.add),
                reads=[K_a, K_g], writes=[K_g])
        P.op(VE, lambda e: e.tensor_scalar(_h[:, 0:W], _a[:, 0:W],
                                           2.5, None, A.is_gt),
             reads=[K_a], writes=[K_h])
        P.op(VE, lambda e: e.scalar_tensor_tensor(
            _h[:, 0:W], _a[:, 0:W], 3.5, _h[:, 0:W], A.is_gt, A.add),
            reads=[K_a, K_h], writes=[K_h])
        P.op(VE, lambda e: e.scalar_tensor_tensor(
            _g[:, 0:W], _g[:, 0:W], 0.5, _h[:, 0:W], A.mult, A.add),
            reads=[K_g, K_h], writes=[K_g])
        P.op(VE, lambda e: e.tensor_scalar(_h[:, 0:W], _a[:, 0:W],
                                           5.0, None, A.is_gt),
             reads=[K_a], writes=[K_h])
        P.op(VE, lambda e: e.scalar_tensor_tensor(
            _g[:, 0:W], _h[:, 0:W], 2.0, _g[:, 0:W], A.mult, A.add),
            reads=[K_h, K_g], writes=[K_g])
        P.op(VE, lambda e: e.tensor_tensor(
            _t[:, 0:W].rearrange("p (b c) -> p b c", c=16),
            _g[:, 0:W].rearrange("p (b c) -> p b c", c=16), scb, A.mult),
            reads=[K_g, K_sc], writes=[K_t])
        P.op(VE, lambda e: e.tensor_scalar(_a[:, 0:W], _u[:, 0:W],
                                           0.0, 2.0, A.is_ge, A.mult),
             reads=[K_u], writes=[K_a])
        P.op(VE, lambda e: e.scalar_tensor_tensor(
            _g[:, 0:W], _a[:, 0:W], 1.0, _t[:, 0:W], A.subtract, A.mult),
            reads=[K_a, K_t], writes=[K_g])
        P.op(VE, lambda e: e.transpose(_u[:, 0:W], _g[:, 0:W]),
             reads=[K_g], writes=[K_u])
        P.op(GP, lambda e: e.dma_start(
            out=dst[r0:r0 + 128, cs:cs + W], in_=_u[:, 0:W]),
            reads=[K_u], dma="sp")


    _qw = []
    for _wsrc, _wdst, _R, _C in [(wg_T, wqg_d, D, II), (wu_T, wqu_d, D, II),
                                 (wd_T, wqd_d, II, D)]:
        for _rt in range(_R // 128):
            _c0 = 0
            while _c0 < _C:
                _W = min(512, _C - _c0)
                _qw.append((_wsrc, _wdst, _rt * 128, _c0, _W))
                _c0 += _W
    N_P1Q = min(12 * NCH1, len(_qw))
    _qw_p1 = _qw[:N_P1Q]
    _qw_rest = _qw[N_P1Q:]
    p1q_slots = (b_at[:, 5120:5632], b_at[:, 5632:6144], b_at[:, 6144:6656],
                 b_at[:, 6656:7168], b_at[:, 7168:7680], b_at[:, 7680:8192],
                 b_ct[:, 64:96], b_ct[:, 96:128])
    p1q_next = [0]

    def p1_qunit(n=1):
        for _ in range(n):
            if p1q_next[0] < len(_qw_p1):
                emit_qtile(*_qw_p1[p1q_next[0]], SL=p1q_slots, NM="pq_",
                           ch="pq0", qeng=SY)
                p1q_next[0] += 1

    x_t = [b_x[:, kt * 512:(kt + 1) * 512] for kt in range(KT)]
    xn_t = [b_xn[:, kt * 512:(kt + 1) * 512] for kt in range(KT)]
    q_t = [b_qk[:, mt * 512:(mt + 1) * 512] for mt in range(MT)]
    k_t = [b_qk[:, 4096 + mt * 512:4096 + (mt + 1) * 512] for mt in range(MT)]
    vt_t = [b_vt[:, i * VTW:(i + 1) * VTW] for i in range(NVT)]
    r_m = b_at[0:1, 0:512]
    r_t = b_at[0:1, 512:1024]
    r_var = b_at[0:1, 1024:1536]
    r_rstd = b_at[0:1, 1536:2048]
    r_mr = b_at[0:1, 2048:2560]
    r_lam = b_at[0:1, 2560:3072]
    r_lam1 = b_at[0:1, 3072:3584]
    r_d = b_at[0:1, 3584:4096]
    lam_rep = b_at[:, 4096:4608]
    lam1_rep = b_at[:, 4608:5120]
    s_a = b_scr[:, 0:512]
    s_b = b_scr[:, 512:1024]
    s_c = b_scr[:, 1024:1536]
    s_d2 = b_scr[:, 1536:2048]
    s_e = b_scr[:, 2048:2560]

    ssets = [
        dict(a=b_scr[:, 0:512], b=b_scr[:, 512:1024], c=b_scr[:, 1024:1536],
             d=b_scr[:, 1536:2048], e=b_scr[:, 2048:2560]),
        dict(a=b_scr[:, 2560:3072], b=b_x[:, 4096:4608],
             c=b_x[:, 4608:5120], d=b_x[:, 5120:5632],
             e=b_x[:, 5632:6144]),
    ]

    def proj_poincare(proj, zt, pco, spill, tcs, tail_hook=None):
        qt = q_t if proj == "q" else k_t

        def emit_mms(mt):
            wb = b_pw[:, (mt % 2) * 1024:(mt % 2) * 1024 + 1024]
            P.op(SY, lambda e, mt=mt, wb=wb, zt=zt: e.dma_start(
                out=wb, in_=zt[mt * 128:(mt + 1) * 128, :]),
                writes=[("pw", mt % 2)], dma=f"zw{mt % 2}")
            pv = psB[:, 1024 + (mt % 2) * 512:1536 + (mt % 2) * 512]
            pvk = ("psB", 2 + mt % 2)
            for kt in range(KT):
                P.op(TE, lambda e, wb=wb, kt=kt, pv=pv: e.matmul(
                    pv, wb[:, kt * 128:(kt + 1) * 128], xn_t[kt],
                    start=(kt == 0), stop=(kt == KT - 1)),
                    reads=[("pw", mt % 2), ("xnt", kt)], writes=[pvk])
            return pv, pvk

        def chain_stages(mt, si, pv, pvk):
            ss = ssets[si]
            sfx = str(si)
            c1 = pc[:, pco + mt:pco + mt + 1]
            s2r = pc[:, pco + 8 + mt:pco + 8 + mt + 1]
            t2 = pc[:, pco + 16 + mt:pco + 16 + mt + 1]
            nt2 = pc[:, pco + 24 + mt:pco + 24 + mt + 1]
            sa, sb2, sc3, sd, se = ss["a"], ss["b"], ss["c"], ss["d"], ss["e"]
            yield lambda: P.op(VE, lambda e: e.scalar_tensor_tensor(
                sa, pv, c1, lam_rep, A.mult, A.mult),
                reads=[pvk, "lamr", "pcc"], writes=["sa" + sfx])
            yield lambda: P.op(VE, lambda e: e.scalar_tensor_tensor(
                sb2, lam1_rep, s2r, sa, A.mult, A.subtract),
                reads=["lam1r", "sa" + sfx, "pcs"], writes=["sb" + sfx])
            yield lambda: P.op(SC, lambda e: e.activation(sc3, sb2, AF.Square),
                               reads=["sb" + sfx], writes=["sc" + sfx])
            yield lambda: P.op(SC, lambda e: e.activation(sc3, sc3, AF.Ln,
                                                          bias=1.0),
                               reads=["sc" + sfx], writes=["sc" + sfx])
            yield lambda: P.op(SC, lambda e: e.activation(sc3, sc3, AF.Exp,
                                                          scale=0.5),
                               reads=["sc" + sfx, "consts"], writes=["sc" + sfx])
            yield lambda: P.op(VE, lambda e: e.tensor_add(sc3, sb2, sc3),
                               reads=["sb" + sfx, "sc" + sfx], writes=["sc" + sfx])
            yield lambda: P.op(SC, lambda e: e.activation(sc3, sc3, AF.Ln),
                               reads=["sc" + sfx], writes=["sc" + sfx])
            yield lambda: P.op(SC, lambda e: e.activation(sd, sc3, AF.Exp,
                                                          scale=t2),
                               reads=["sc" + sfx, "pct"], writes=["sd" + sfx])
            yield lambda: P.op(SC, lambda e: e.activation(se, sc3, AF.Exp,
                                                          scale=nt2),
                               reads=["sc" + sfx, "pcn"], writes=["se" + sfx])
            yield lambda: P.op(VE, lambda e: e.tensor_sub(qt[mt], sd, se),
                               reads=["sd" + sfx, "se" + sfx],
                               writes=[("qt", (proj, mt))])
            yield lambda: P.op(SC, lambda e: e.activation(sa, qt[mt],
                                                          AF.Square),
                               reads=[("qt", (proj, mt))], writes=["sa" + sfx])
            yield lambda: P.op(TE, lambda e, mt=mt: e.matmul(
                psB[0:1, 0:512], ones_col, sa,
                start=(mt == 0), stop=(mt == MT - 1)),
                reads=["sa" + sfx, "ones"], writes=[("psB", 0)])

        pending_tail = []
        for b0 in range(0, MT, 2):
            pv0, pvk0 = emit_mms(b0)
            pv1, pvk1 = emit_mms(b0 + 1)
            if b0 == 0 and tail_hook is not None:
                tail_hook()
            for t0, t1 in pending_tail:
                t0()
                t1()
            pending_tail = []
            g0 = list(chain_stages(b0, 0, pv0, pvk0))
            g1 = list(chain_stages(b0 + 1, 1, pv1, pvk1))
            for st0, st1 in zip(g0[:-2], g1[:-2]):
                st0()
                st1()
            pending_tail = [(g0[-2], g1[-2]), (g0[-1], g1[-1])]
        for t0, t1 in pending_tail:
            t0()
            t1()
        rd_ps = psA[:, 0:512] if proj == "q" else psB[:, 0:512]
        rd_key = ("psA", 0) if proj == "q" else ("psB", 0)

        def tail():
            P.op(SC, lambda e: e.activation(r_d, psB[0:1, 0:512], AF.Ln, bias=4.0),
                 reads=[("psB", 0), "consts"], writes=["r_d"])
            P.op(SC, lambda e: e.activation(r_d, r_d, AF.Exp, scale=0.5),
                 reads=["r_d", "consts"], writes=["r_d"])
            P.op(VE, lambda e: e.tensor_scalar(r_d, r_d, 2.0, None, A.add),
                 reads=["r_d"], writes=["r_d"])
            P.op(VE, lambda e: e.reciprocal(r_d, r_d), reads=["r_d"], writes=["r_d"])
            P.op(TE, lambda e: e.matmul(rd_ps, ones_row, r_d,
                                        start=True, stop=True),
                 reads=["r_d", "ones"], writes=[rd_key])
            for mt in range(MT):
                P.op(VE, lambda e, qt=qt, mt=mt: e.tensor_tensor(
                    qt[mt], qt[mt], rd_ps, A.mult),
                    reads=[("qt", (proj, mt)), rd_key],
                    writes=[("qt", (proj, mt))])
                P.op(GP, lambda e, qt=qt, mt=mt, spill=spill, tcs=tcs: e.dma_start(
                    out=spill[mt * 128:(mt + 1) * 128, tcs], in_=qt[mt]),
                    reads=[("qt", (proj, mt))], dma="sp")
        return tail


    scr_sq = b_x[:, 6144:6656]

    def p1_xload(tc):
        tcs = slice(tc * 512, (tc + 1) * 512)
        for kt in range(KT):
            P.op(SY, lambda e, kt=kt, tcs=tcs: e.dma_start(
                out=x_t[kt], in_=x_fm[kt * 128:(kt + 1) * 128, tcs]),
                writes=[("xt", kt)], dma="xl")

    def p1_stats():
        for kt in range(KT):
            P.op(SC, lambda e, kt=kt: e.activation(scr_sq, x_t[kt], AF.Square),
                 reads=[("xt", kt)], writes=["sqx"])
            P.op(TE, lambda e, kt=kt: e.matmul(psB[0:1, 0:512], ones_col, scr_sq,
                                               start=(kt == 0), stop=(kt == KT - 1)),
                 reads=["sqx", "ones"], writes=[("psB", 0)])
            P.op(TE, lambda e, kt=kt: e.matmul(psB[0:1, 512:1024], ones_col,
                                               x_t[kt],
                                               start=(kt == 0), stop=(kt == KT - 1)),
                 reads=[("xt", kt), "ones"], writes=[("psB", 1)])

    def p1_rows():
        P.op(VE, lambda e: e.tensor_scalar(r_m, psB[0:1, 512:1024], 1.0 / D,
                                           None, A.mult),
             reads=[("psB", 1)], writes=["r_m"])
        P.op(VE, lambda e: e.tensor_mul(r_t, r_m, r_m), reads=["r_m"], writes=["r_t"])
        P.op(VE, lambda e: e.scalar_tensor_tensor(r_var, psB[0:1, 0:512],
                                                  1.0 / D, r_t, A.mult, A.subtract),
             reads=[("psB", 0), "r_t"], writes=["r_var"])
        P.op(SC, lambda e: e.activation(r_rstd, r_var, AF.Ln, bias=1e-5),
             reads=["r_var", "consts"], writes=["r_rstd"])
        P.op(SC, lambda e: e.activation(r_rstd, r_rstd, AF.Exp, scale=-0.5),
             reads=["r_rstd", "consts"], writes=["r_rstd"])
        P.op(VE, lambda e: e.tensor_mul(r_mr, r_m, r_rstd),
             reads=["r_m", "r_rstd"], writes=["r_mr"])
        P.op(VE, lambda e: e.tensor_scalar(r_t, r_var, 1e-5, None, A.add),
             reads=["r_var"], writes=["r_t"])
        P.op(VE, lambda e: e.reciprocal(r_t, r_t), reads=["r_t"], writes=["r_t"])
        P.op(VE, lambda e: e.tensor_mul(r_t, r_var, r_t),
             reads=["r_var", "r_t"], writes=["r_t"])
        P.op(VE, lambda e: e.tensor_scalar(r_t, r_t, -float(D), 1.0, A.mult, A.add),
             reads=["r_t"], writes=["r_t"])
        P.op(VE, lambda e: e.reciprocal(r_t, r_t), reads=["r_t"], writes=["r_t"])
        P.op(VE, lambda e: e.tensor_scalar(r_lam, r_t, 2.0, None, A.mult),
             reads=["r_t"], writes=["r_lam"])
        P.op(VE, lambda e: e.tensor_scalar(r_lam1, r_lam, -1.0, None, A.add),
             reads=["r_lam"], writes=["r_lam1"])

    def p1_reps_xn():
        P.op(TE, lambda e: e.matmul(psA[:, 0:512], ones_row, r_rstd,
                                    start=True, stop=True),
             reads=["r_rstd", "ones"], writes=[("psA", 0)])
        P.op(TE, lambda e: e.matmul(psA[:, 512:1024], ones_row, r_mr,
                                    start=True, stop=True),
             reads=["r_mr", "ones"], writes=[("psA", 1)])
        P.op(TE, lambda e: e.matmul(psA[:, 1024:1536], ones_row, r_lam,
                                    start=True, stop=True),
             reads=["r_lam", "ones"], writes=[("psA", 2)])
        P.op(TE, lambda e: e.matmul(psA[:, 1536:2048], ones_row, r_lam1,
                                    start=True, stop=True),
             reads=["r_lam1", "ones"], writes=[("psA", 3)])
        P.op(VE, lambda e: e.tensor_copy(lam_rep, psA[:, 1024:1536]),
             reads=[("psA", 2)], writes=["lamr"])
        P.op(VE, lambda e: e.tensor_copy(lam1_rep, psA[:, 1536:2048]),
             reads=[("psA", 3)], writes=["lam1r"])
        for kt in range(KT):
            P.op(VE, lambda e, kt=kt: e.tensor_tensor(xn_t[kt], x_t[kt],
                                                      psA[:, 0:512], A.mult),
                 reads=[("xt", kt), ("psA", 0)], writes=[("xnt", kt)])
            P.op(VE, lambda e, kt=kt: e.tensor_tensor(xn_t[kt], xn_t[kt],
                                                      psA[:, 512:1024], A.subtract),
                 reads=[("xnt", kt), ("psA", 1)], writes=[("xnt", kt)])

    def p1_v_wave(tc, wave, hook=None):
        for kt in range(KT):
            if kt == 2 and hook is not None:
                hook()
            wb = b_pw[:, (kt % 2) * 1024:(kt % 2) * 1024 + 1024]
            P.op(SY, lambda e, kt=kt, wb=wb: e.dma_start(
                out=wb, in_=wv_T[kt * 128:(kt + 1) * 128, :]),
                writes=[("pw", kt % 2)], dma=f"zw{kt % 2}")
            for tt in (wave * 2, wave * 2 + 1):
                for nh2 in range(2):
                    bank = (tt % 2) * 2 + nh2
                    pvv = psA[:, bank * 512:(bank + 1) * 512]
                    P.op(TE, lambda e, wb=wb, kt=kt, tt=tt, nh2=nh2, pvv=pvv:
                         e.matmul(
                             pvv, xn_t[kt][:, tt * 128:(tt + 1) * 128],
                             wb[:, nh2 * 512:(nh2 + 1) * 512],
                             start=(kt == 0), stop=(kt == KT - 1)),
                         reads=[("pw", kt % 2), ("xnt", kt)],
                         writes=[("psA", bank)])
        for tt in (wave * 2, wave * 2 + 1):
            vi = tc * 4 + tt
            P.op(VE, lambda e, vi=vi: e.memset(
                vt_t[vi].rearrange("p (h w) -> p h w", w=65)[:, :, 64:65], 1.0),
                writes=[("vt", vi)])
            for nh2 in range(2):
                bank = (tt % 2) * 2 + nh2
                pvv = psA[:, bank * 512:(bank + 1) * 512]
                P.op(VE, lambda e, vi=vi, nh2=nh2, pvv=pvv: e.scalar_tensor_tensor(
                    vt_t[vi].rearrange("p (h w) -> p h w", w=65)
                    [:, nh2 * 8:(nh2 + 1) * 8, 0:64],
                    pvv.rearrange("p (h d) -> p h d", d=64), 1.0,
                    bv_rep[:, nh2 * 512:(nh2 + 1) * 512]
                    .rearrange("p (h d) -> p h d", d=64),
                    A.mult, A.add),
                    reads=[("psA", bank), "bvrep"], writes=[("vt", vi)])

    p1_xload(0)
    p1_stats()
    p1_rows()
    for tc in range(NCH1):
        tcs = slice(tc * 512, (tc + 1) * 512)
        p1_reps_xn()
        tq_tail = proj_poincare("q", zq_t, 0, q_sp, tcs)
        p1_qunit(3)
        tk_tail = proj_poincare("k", zk_t, 32, k_sp, tcs, tail_hook=tq_tail)
        if tc + 1 < NCH1:
            p1_xload(tc + 1)
        p1_qunit(3)
        p1_v_wave(tc, 0, hook=tk_tail)
        if tc + 1 < NCH1:
            p1_stats()
        p1_qunit(3)
        p1_v_wave(tc, 1)
        p1_qunit(3)
        if tc + 1 < NCH1:
            p1_rows()
        for kt in range(KT):
            P.op(GP, lambda e, kt=kt, tcs=tcs: e.dma_start(
                out=xn_sp[kt * 128:(kt + 1) * 128, tcs], in_=xn_t[kt]),
                reads=[("xnt", kt)], dma="sp")

    P.barrier()

    SC2 = 512
    NC2 = S // SC2
    K2buf = [b_qk[:, 0:S], b_qk[:, S:2 * S]]
    Q2buf = [b_qk[:, 2 * S:2 * S + 512], b_qk[:, 2 * S + 512:2 * S + 1024]]
    eslots = []
    for buf, c0, c1 in [(b_x, 0, 7040), (b_qk, 2 * S + 1024, 8192),
                        (b_pw, 0, 2048), (b_at, 4096, 8192),
                        (b_xn, 2560, 4096)]:
        c = c0
        while c + 512 <= c1:
            eslots.append(buf[:, c:c + 512])
            c += 512
    assert len(eslots) >= 2 * NVT, (len(eslots), NVT)
    Et = [[eslots[p * NVT + tk] for tk in range(NVT)] for p in range(2)]
    at_t = [b_at[:, mt * 512:(mt + 1) * 512] for mt in range(MT)]
    rs_t = b_xn[0:16, 0:512]
    rs_scr = [b_xn[64:65, 1536:2048], b_xn[64:65, 2048:2560]]

    qunits = list(_qw_rest)
    for tc in range(NCH1):
        for kt in range(KT):
            qunits.append((xn_sp, xq_sp, kt * 128, tc * 512, 512))

    per_iter = (len(qunits) + NC2 * 8 - 1) // (NC2 * 8)
    q_next = [0]

    def emit_unit():
        if q_next[0] < len(qunits):
            emit_qtile(*qunits[q_next[0]])
            q_next[0] += 1
            return True
        return False

    KP = max(1, NVT // 4)

    def load_pair(ch, hp):
        chs2 = slice(ch * SC2, (ch + 1) * SC2)
        P.op(SY, lambda e, hp=hp, chs2=chs2: e.dma_start(
            out=Q2buf[hp % 2], in_=q_sp[hp * 128:(hp + 1) * 128, chs2]),
            writes=[("qb", hp % 2)], dma=f"q{hp % 2}")
        for pi in range(NVT // KP):
            csl = slice(pi * KP * 128, (pi + 1) * KP * 128)
            P.op(SY, lambda e, hp=hp, csl=csl: e.dma_start(
                out=K2buf[hp % 2][:, csl], in_=k_sp[hp * 128:(hp + 1) * 128, csl]),
                writes=[("kb", (hp % 2, pi))], dma=f"k{hp % 2}p{pi}")

    load_pair(0, 0)
    for ch in range(NC2):
        chs = slice(ch * SC2, (ch + 1) * SC2)
        for hp in range(8):
            if hp < 7:
                load_pair(ch, hp + 1)
            elif ch + 1 < NC2:
                load_pair(ch + 1, 0)
            for _ in range(per_iter):
                emit_unit()
            for tk in range(NVT):
                for par in range(2):
                    bank = par * 2 + (tk % 2)
                    P.op(TE, lambda e, hp=hp, tk=tk, par=par, bank=bank: e.matmul(
                        psA[:, bank * 512:(bank + 1) * 512],
                        K2buf[hp % 2][par * 64:(par + 1) * 64,
                                      tk * 128:(tk + 1) * 128],
                        Q2buf[hp % 2][par * 64:(par + 1) * 64, :],
                        start=True, stop=True),
                        reads=[("kb", (hp % 2, tk // KP)), ("qb", hp % 2)],
                        writes=[("psA", bank)])
                for par in range(2):
                    bank = par * 2 + (tk % 2)
                    P.op(SC, lambda e, tk=tk, par=par, bank=bank: e.activation(
                        Et[par][tk], psA[:, bank * 512:(bank + 1) * 512],
                        AF.Exp, scale=0.125),
                        reads=[("psA", bank), "consts"],
                        writes=[("eb", (par, tk))])
            for tk in range(NVT):
                for par in range(2):
                    P.op(TE, lambda e, hp=hp, tk=tk, par=par: e.matmul(
                        psB[0:65, par * 512:(par + 1) * 512],
                        vt_t[tk][:, (2 * hp + par) * 65:(2 * hp + par) * 65 + 65],
                        Et[par][tk],
                        start=(tk == 0), stop=(tk == NVT - 1)),
                        reads=[("eb", (par, tk)), ("vt", tk)],
                        writes=[("psB", par)])
            for par in range(2):
                P.op(SC, lambda e, hp=hp, par=par: e.activation(
                    at_t[hp][par * 64:(par + 1) * 64, :],
                    psB[0:64, par * 512:(par + 1) * 512], AF.Copy),
                    reads=[("psB", par)], writes=[("at", hp)])
                P.op(SC, lambda e, par=par: e.activation(
                    rs_scr[par], psB[64:65, par * 512:(par + 1) * 512],
                    AF.Copy),
                    reads=[("psB", par)], writes=[("rss", par)])
                P.op(SY, lambda e, hp=hp, par=par: e.dma_start(
                    out=rs_t[2 * hp + par:2 * hp + par + 1, :],
                    in_=rs_scr[par]),
                    reads=[("rss", par)], writes=[("rs", par)],
                    dma=f"rs{par}")
            P.op(SY, lambda e, hp=hp, chs=chs: e.dma_start(
                out=at_sp[hp * 128:(hp + 1) * 128, chs], in_=at_t[hp]),
                reads=[("at", hp)], dma="at")
        P.op(SY, lambda e, chs=chs: e.dma_start(
            out=rs_dr[:, chs], in_=rs_t), reads=["rs"], dma="rsd")
    while emit_unit():
        pass

    P.barrier()

    xq_t = [b_vt[:, kt * S:(kt + 1) * S] for kt in range(KT)]
    for kt in range(KT):
        P.op(SY, lambda e, kt=kt: e.dma_start(
            out=xq_t[kt], in_=xq_sp[kt * 128:(kt + 1) * 128, :]),
            writes=[("xqt", kt)], dma="xl")
    sg_t = b_xn[:, 0:S]
    h_t = b_xn[:, 2048:2048 + S] if S <= 2048 else None
    wgbuf = [b_qk[:, 0:1024], b_qk[:, 1024:2048]]
    wubuf = [b_qk[:, 2048:3072], b_qk[:, 3072:4096]]
    hq_t = b_qk[:, 4096:4096 + S]
    hbm = b_ct[:, 0:64]
    hsc2 = b_ct[:, 64:128]
    for m in range(IT):
        P.op(SY, lambda e, m=m: e.dma_start(
            out=wgbuf[m % 2].rearrange("p (kt c) -> p kt c", c=128),
            in_=wqg_d.rearrange("(kt p) o -> p kt o", p=128)
            [:, :, m * 128:(m + 1) * 128]),
            writes=[("wgb", m % 2)], dma=f"wf{m % 2}")
        P.op(SY, lambda e, m=m: e.dma_start(
            out=wubuf[m % 2].rearrange("p (kt c) -> p kt c", c=128),
            in_=wqu_d.rearrange("(kt p) o -> p kt o", p=128)
            [:, :, m * 128:(m + 1) * 128]),
            writes=[("wub", m % 2)], dma=f"wf{m % 2}")
        for kt in range(KT):
            for n in range(NTQ):
                P.op(TE, lambda e, m=m, kt=kt, n=n: e.matmul(
                    psA[:, n * 512:(n + 1) * 512],
                    wgbuf[m % 2][:, kt * 128:(kt + 1) * 128],
                    xq_t[kt][:, n * 512:(n + 1) * 512],
                    start=(kt == 0), stop=(kt == KT - 1)),
                    reads=[("wgb", m % 2), ("xqt", kt)], writes=[("psA", n)])
                P.op(TE, lambda e, m=m, kt=kt, n=n: e.matmul(
                    psB[:, n * 512:(n + 1) * 512],
                    wubuf[m % 2][:, kt * 128:(kt + 1) * 128],
                    xq_t[kt][:, n * 512:(n + 1) * 512],
                    start=(kt == 0), stop=(kt == KT - 1)),
                    reads=[("wub", m % 2), ("xqt", kt)], writes=[("psB", n)])
        P.op(SC, lambda e: e.activation(sg_t, psA[:, 0:S], AF.Silu),
             reads=[("psA", n) for n in range(NTQ)], writes=["sg"])
        P.op(VE, lambda e: e.scalar_tensor_tensor(h_t, psB[:, 0:S], 1.0, sg_t,
                                                  A.mult, A.mult),
             reads=[("psB", n) for n in range(NTQ)] + ["sg"], writes=["h"])
        for hf in range(S // 1024 if S >= 1024 else 1):
            HW = min(1024, S)
            hsl = h_t[:, hf * HW:(hf + 1) * HW]
            t32 = b_scr[:, 0:HW]
            u = b_scr[:, 1024:1024 + HW]
            aa = b_scr[:, 2048:2048 + HW]
            g = b_at[:, 0:HW]
            h2 = b_at[:, 1024:1024 + HW]
            nb = HW // 16
            P.op(VE, lambda e, hsl=hsl, t32=t32: e.transpose(t32, hsl),
                 reads=["h"], writes=["hin"])
            P.op(VE, lambda e, t32=t32, nb=nb: e.reduce_max(
                hbm[:, 0:nb], t32.rearrange("p (b c) -> p b c", c=16), AX.X,
                apply_absolute_value=True),
                reads=["hin"], writes=["hbm"])
            P.op(VE, lambda e, nb=nb: e.tensor_scalar(
                hsc2[:, 0:nb], hbm[:, 0:nb], 1.0 / 6.0, 1e-12, A.mult, A.max),
                reads=["hbm"], writes=["hsc"])
            P.op(VE, lambda e, nb=nb: e.reciprocal(hbm[:, 0:nb], hsc2[:, 0:nb]),
                 reads=["hsc"], writes=["hbm"])
            scb = hsc2[:, 0:nb].rearrange("p (b o) -> p b o", o=1) \
                .broadcast_to([128, nb, 16])
            rcb = hbm[:, 0:nb].rearrange("p (b o) -> p b o", o=1) \
                .broadcast_to([128, nb, 16])
            P.op(VE, lambda e, u=u, t32=t32, rcb=rcb: e.tensor_tensor(
                u.rearrange("p (b c) -> p b c", c=16),
                t32.rearrange("p (b c) -> p b c", c=16), rcb, A.mult),
                reads=["hin", "hbm"], writes=["hu"])
            P.op(VE, lambda e, u=u, aa=aa: e.scalar_tensor_tensor(
                aa, u, -1.0, u, A.mult, A.max), reads=["hu"], writes=["ha"])
            P.op(VE, lambda e, aa=aa, g=g: e.tensor_scalar(
                g, aa, 0.25, None, A.is_gt), reads=["ha"], writes=["hg"])
            for th in (0.75, 1.25, 1.75):
                P.op(VE, lambda e, aa=aa, g=g, th=th: e.scalar_tensor_tensor(
                    g, aa, th, g, A.is_gt, A.add),
                    reads=["ha", "hg"], writes=["hg"])
            P.op(VE, lambda e, aa=aa, h2=h2: e.tensor_scalar(
                h2, aa, 2.5, None, A.is_gt), reads=["ha"], writes=["hh"])
            P.op(VE, lambda e, aa=aa, h2=h2: e.scalar_tensor_tensor(
                h2, aa, 3.5, h2, A.is_gt, A.add), reads=["ha", "hh"], writes=["hh"])
            P.op(VE, lambda e, g=g, h2=h2: e.scalar_tensor_tensor(
                g, g, 0.5, h2, A.mult, A.add), reads=["hg", "hh"], writes=["hg"])
            P.op(VE, lambda e, aa=aa, h2=h2: e.tensor_scalar(
                h2, aa, 5.0, None, A.is_gt), reads=["ha"], writes=["hh"])
            P.op(VE, lambda e, g=g, h2=h2: e.scalar_tensor_tensor(
                g, h2, 2.0, g, A.mult, A.add), reads=["hh", "hg"], writes=["hg"])
            P.op(VE, lambda e, t32=t32, g=g, scb=scb: e.tensor_tensor(
                t32.rearrange("p (b c) -> p b c", c=16),
                g.rearrange("p (b c) -> p b c", c=16), scb, A.mult),
                reads=["hg", "hsc"], writes=["hin"])
            P.op(SC, lambda e, u=u, aa=aa: e.sign(aa, u),
                 reads=["hu"], writes=["ha"])
            P.op(VE, lambda e, g=g, aa=aa, t32=t32: e.tensor_mul(g, t32, aa),
                 reads=["hin", "ha", "hg"], writes=["hg"])
            P.op(VE, lambda e, g=g, hf=hf, HW=HW: e.transpose(
                hq_t[:, hf * HW:(hf + 1) * HW], g), reads=["hg"], writes=["hqo"])
        P.op(GP, lambda e, m=m: e.dma_start(
            out=hq_sp[m * 128:(m + 1) * 128, :], in_=hq_t),
            reads=["hqo"], dma="sp")

    P.barrier()

    hqbuf = [b_vt[:, i * 512:(i + 1) * 512] for i in range(IT)]
    wdbuf = [b_qk[:, 0:1024], b_qk[:, 1024:2048]]
    att_s = [b_scr[:, 0:512], b_scr[:, 1536:2048]]
    x_s = [b_scr[:, 512:1024], b_scr[:, 2048:2560]]
    ob = b_scr[:, 1024:1536]
    def c2_hq(n, kt):
        ns2 = slice(n * 512, (n + 1) * 512)
        P.op(SY, lambda e, kt=kt, ns2=ns2: e.dma_start(
            out=hqbuf[kt], in_=hq_sp[kt * 128:(kt + 1) * 128, ns2]),
            writes=[("hqb", kt)], dma=f"hq{kt % 4}")

    def c2_wd(kt):
        P.op(SY, lambda e, kt=kt: e.dma_start(
            out=wdbuf[kt % 2], in_=wqd_d[kt * 128:(kt + 1) * 128, :]),
            writes=[("wdb", kt % 2)], dma=f"wf{kt % 2}")

    rs_rep = [b_at[:, 0:512], b_at[:, 512:1024]]
    obs = [ob, b_scr[:, 2560:3072]]

    def c2_rb(n, m):
        ns2 = slice(n * 512, (n + 1) * 512)
        for par in range(2):
            P.op(SY, lambda e, m=m, par=par, ns2=ns2: e.dma_start(
                out=rs_rep[m % 2][par * 64:(par + 1) * 64, :],
                in_=rs_dr[2 * m + par:2 * m + par + 1,
                          ns2].broadcast_to([64, 512])),
                writes=[("rsr", m % 2)], dma=f"rb{m % 2}")

    c2_hq(0, 0)
    c2_hq(0, 1)
    c2_wd(0)
    c2_rb(0, 0)
    for n in range(NTQ):
        ns = slice(n * 512, (n + 1) * 512)
        for kt in range(IT):
            if kt + 2 < IT:
                c2_hq(n, kt + 2)
            elif n + 1 < NTQ:
                c2_hq(n + 1, kt + 2 - IT)
            if kt + 1 < IT:
                c2_wd(kt + 1)
            elif n + 1 < NTQ:
                c2_wd(0)
            for m in range(MT):
                pp = psA if m < 4 else psB
                pk = ("psA" if m < 4 else "psB", m % 4)
                P.op(TE, lambda e, kt=kt, m=m, pp=pp: e.matmul(
                    pp[:, (m % 4) * 512:(m % 4) * 512 + 512],
                    wdbuf[kt % 2][:, m * 128:(m + 1) * 128], hqbuf[kt],
                    start=(kt == 0), stop=(kt == IT - 1)),
                    reads=[("wdb", kt % 2), ("hqb", kt)], writes=[pk])
        for m in range(MT):
            pp = psA if m < 4 else psB
            pk = ("psA" if m < 4 else "psB", m % 4)
            P.op(SY, lambda e, m=m, ns=ns: e.dma_start(
                out=att_s[m % 2], in_=at_sp[m * 128:(m + 1) * 128, ns]),
                writes=[("atts", m % 2)], dma=f"as{m % 2}")
            P.op(SY, lambda e, m=m, ns=ns: e.dma_start(
                out=x_s[m % 2], in_=x_fm[m * 128:(m + 1) * 128, ns]),
                writes=[("xs", m % 2)], dma=f"xs{m % 2}")
            if m + 1 < MT:
                c2_rb(n, m + 1)
            elif n + 1 < NTQ:
                c2_rb(n + 1, 0)
            P.op(VE, lambda e, m=m: e.reciprocal(rs_rep[m % 2],
                                                 rs_rep[m % 2]),
                 reads=[("rsr", m % 2)], writes=[("rsr", m % 2)])
            P.op(VE, lambda e, m=m: e.tensor_tensor(
                obs[m % 2], att_s[m % 2], rs_rep[m % 2], A.mult),
                reads=[("atts", m % 2), ("rsr", m % 2)], writes=[("ob", m % 2)])
            P.op(VE, lambda e, m=m, pp=pp: e.scalar_tensor_tensor(
                obs[m % 2], pp[:, (m % 4) * 512:(m % 4) * 512 + 512], 1.0,
                obs[m % 2], A.mult, A.add),
                reads=[pk, ("ob", m % 2)], writes=[("ob", m % 2)])
            P.op(VE, lambda e, m=m: e.tensor_add(obs[m % 2], obs[m % 2],
                                                 x_s[m % 2]),
                 reads=[("ob", m % 2), ("xs", m % 2)], writes=[("ob", m % 2)])
            P.op(GP, lambda e, m=m, ns=ns: e.dma_start(
                out=out_fm[m * 128:(m + 1) * 128, ns], in_=obs[m % 2]),
                reads=[("ob", m % 2)], dma="out")

    P.final_wait(SY)

    with contextlib.ExitStack() as st:
        sems = {}
        for i, name in enumerate(sorted(P.cnt)):
            sems[name] = st.enter_context(nc.semaphore(f"s{i}_{name[:8]}"))
        blk = st.enter_context(nc.Block())
        P.emit(blk, sems)

    return nc


def _layout_inputs(ins, S=2048):
    x = np.ascontiguousarray(ins["L0_attn_x"])
    z_q = np.ascontiguousarray(ins["L0_attn_z_q"])
    z_k = np.ascontiguousarray(ins["L0_attn_z_k"])
    w_v = np.ascontiguousarray(ins["L0_attn_w_v"])
    b_q = np.ascontiguousarray(ins["L0_attn_b_q"])
    b_k = np.ascontiguousarray(ins["L0_attn_b_k"])
    b_v = np.ascontiguousarray(ins["L0_attn_b_v"])
    w_g = np.ascontiguousarray(ins["L0_ffn_w_gate"])
    w_u = np.ascontiguousarray(ins["L0_ffn_w_up"])
    w_d = np.ascontiguousarray(ins["L0_ffn_w_down"])

    def tile_mtkt(z):
        return np.ascontiguousarray(
            z.reshape(KT, 128, MT, 128).transpose(2, 1, 0, 3).reshape(D, D))

    shared = {
        "zq_t": tile_mtkt(z_q),
        "zk_t": tile_mtkt(z_k),
        "wv_T": np.ascontiguousarray(w_v.T),
        "wg_T": np.ascontiguousarray(w_g.T),
        "wu_T": np.ascontiguousarray(w_u.T),
        "wd_T": np.ascontiguousarray(w_d.T),
        "bqk_pp": np.ascontiguousarray(
            np.concatenate([b_q.reshape(8, 128).T, b_k.reshape(8, 128).T],
                           axis=1)).astype(np.float32),
        "bv_row": np.ascontiguousarray(b_v.reshape(1, D)),
        "sel64": np.ascontiguousarray(
            (np.arange(16)[:, None] == (np.arange(D)[None, :] // 64))
            .astype(np.float32)),
        "eye": np.eye(128, dtype=np.float32),
        "consts": np.tile(np.array([[1e-5, -0.5, 0.5, 2.0, -2.0, 4.0, 0.125,
                                     -1.0]], dtype=np.float32), (128, 1)),
    }
    in_maps = []
    for b in range(x.shape[0]):
        m = dict(shared)
        m["x_fm"] = np.ascontiguousarray(x[b, :S, :].T)
        in_maps.append(m)
    return in_maps


_CACHE = {}


def kernel(**inputs):
    if "nc" not in _CACHE:
        _CACHE["nc"] = build()
    nc = _CACHE["nc"]
    in_maps = _layout_inputs(inputs)
    res = run_bass_kernel_spmd(nc, in_maps, core_ids=list(range(NCORE)))
    out = np.stack([np.asarray(r["out_fm"]).T for r in res.results], axis=0)
    return out.astype(np.float32)


# version 23
# speedup vs baseline: 1.0697x; 1.0697x over previous
import contextlib
import numpy as np
import concourse.bass as bass
import concourse.mybir as mybir
from concourse.bass_utils import run_bass_kernel_spmd

F32 = mybir.dt.float32
A = mybir.AluOpType
AF = mybir.ActivationFunctionType
AX = mybir.AxisListType

D = 1024
NH = 16
DK = 64
II = 2816
NCORE = 8
KT = 8
MT = 8
IT = 22
VTW = NH * 65

SY, TE, VE, SC, GP = "sync", "tensor", "vector", "scalar", "gpsimd"


class Prog:

    ENGINES = (SY, TE, VE, SC, GP)

    def __init__(self):
        self.ops = {e: [] for e in self.ENGINES}
        self.cnt = {e: 0 for e in self.ENGINES}
        self.state = {}
        self.regions = {}

    def _rdeps(self, t, r):
        deps = []
        if r is None:
            for reg in self.regions.get(t, ()):
                w = self.state.get((t, reg), {}).get("w")
                if w:
                    deps.append(w)
        else:
            for reg in (r, None):
                w = self.state.get((t, reg), {}).get("w")
                if w:
                    deps.append(w)
        return deps

    def _wdeps(self, t, r):
        deps = self._rdeps(t, r)
        if r is None:
            for reg in self.regions.get(t, ()):
                deps.extend(self.state.get((t, reg), {}).get("r", ()))
        else:
            for reg in (r, None):
                deps.extend(self.state.get((t, reg), {}).get("r", ()))
        return deps

    def op(self, eng, fn, reads=(), writes=(), dma=None):
        norm = lambda x: x if isinstance(x, tuple) else (x, None)
        reads = [norm(x) for x in reads]
        writes = [norm(x) for x in writes]
        deps = []
        for t, r in reads:
            deps.extend(self._rdeps(t, r))
        for t, r in writes:
            deps.extend(self._wdeps(t, r))
        waits = {}
        for s, v in deps:
            waits[s] = max(waits.get(s, 0), v)
        for s in list(waits):
            if s not in self.ENGINES:
                waits[s] = self.cnt.get(s, 0)
        sem = dma if dma else eng
        inc = 16 if dma else 1
        self.cnt[sem] = self.cnt.get(sem, 0) + inc
        my = (sem, self.cnt[sem])
        self.ops[eng].append((waits, fn, sem, inc))
        for t, r in reads:
            self.regions.setdefault(t, set()).add(r)
            self.state.setdefault((t, r), {"w": None, "r": []})["r"].append(my)
        for t, r in writes:
            self.regions.setdefault(t, set()).add(r)
            ent = self.state.setdefault((t, r), {"w": None, "r": []})
            ent["w"] = my
            ent["r"] = []
        return my

    def barrier(self):
        snap = dict(self.cnt)
        for e in self.ENGINES:
            waits = {s: v for s, v in snap.items() if v > 0}
            self.ops[e].append((waits, None, e, 1))
            self.cnt[e] += 1
        snap2 = {e: self.cnt[e] for e in self.ENGINES}
        for e in self.ENGINES:
            waits = {s: snap2[s] for s in self.ENGINES if s != e}
            self.ops[e].append((waits, None, e, 1))
            self.cnt[e] += 1
        self.state = {}
        self.regions = {}

    def final_wait(self, eng):
        waits = {s: v for s, v in self.cnt.items()
                 if s not in self.ENGINES and v > 0}
        self.ops[eng].append((waits, None, eng, 0))

    def emit(self, blk, sems):
        def runner(name):
            hwm = {}

            def body(eng):
                for waits, fn, sem, inc in self.ops[name]:
                    for s in sorted(waits):
                        v = waits[s]
                        if hwm.get(s, -1) >= v:
                            continue
                        eng.wait_ge(sems[s], v)
                        hwm[s] = v
                    if fn is None:
                        if inc:
                            eng.nop(nofuse=True).then_inc(sems[sem], inc)
                        continue
                    fn(eng).then_inc(sems[sem], inc)
            return body

        blk.sync(runner(SY))
        blk.tensor(runner(TE))
        blk.vector(runner(VE))
        blk.scalar(runner(SC))
        blk.gpsimd(runner(GP))


def bankA(c0, c1):
    return [("psA", b) for b in range(c0 // 512, (c1 + 511) // 512)]


def bankB(c0, c1):
    return [("psB", b) for b in range(c0 // 512, (c1 + 511) // 512)]


def build(S=2048):
    assert S % 512 == 0
    NCH1 = S // 512
    S2 = min(1024, S)
    NCH2 = S // S2
    NVT = S // 128
    NTQ = S // 512

    nc = bass.Bass("TRN2", target_bir_lowering=False)

    din = lambda n, sh: nc.dram_tensor(n, sh, F32, kind="ExternalInput").ap()
    dtmp = lambda n, sh: nc.dram_tensor(n, sh, F32, kind="Internal").ap()

    x_fm = din("x_fm", [D, S])
    zq_t = din("zq_t", [D, D])
    zk_t = din("zk_t", [D, D])
    wv_T = din("wv_T", [D, D])
    wg_T = din("wg_T", [D, II])
    wu_T = din("wu_T", [D, II])
    wd_T = din("wd_T", [II, D])
    bqk_pp = din("bqk_pp", [128, 16])
    bv_row = din("bv_row", [1, D])
    sel64 = din("sel64", [16, D])
    eye = din("eye", [128, 128])
    consts = din("consts", [128, 8])
    out_fm = nc.dram_tensor("out_fm", [D, S], F32, kind="ExternalOutput").ap()

    q_sp = dtmp("q_sp", [D, S])
    k_sp = dtmp("k_sp", [D, S])
    xq_sp = dtmp("xq_sp", [D, S])
    xn_sp = dtmp("xn_sp", [D, S])
    at_sp = dtmp("at_sp", [D, S])
    wqg_d = dtmp("wqg_d", [D, II])
    wqu_d = dtmp("wqu_d", [D, II])
    wqd_d = dtmp("wqd_d", [II, D])
    hq_sp = dtmp("hq_sp", [II, S])
    rs_dr = dtmp("rs_dr", [16, S])

    P = Prog()

    ap_of = {}

    def sbuf(name, shape):
        h = nc.alloc_sbuf_tensor(name, shape, F32)
        ap_of[name] = h.ap()
        return ap_of[name]

    b_x = sbuf("b_x", [128, 7040])
    b_xn = sbuf("b_xn", [128, 4096])
    b_qk = sbuf("b_qk", [128, 8192])
    b_vt = sbuf("b_vt", [128, max(NVT * VTW, 8 * S, IT * 512)])
    b_pw = sbuf("b_pw", [128, 2048])
    b_scr = sbuf("b_scr", [128, 3072])
    b_at = sbuf("b_at", [128, 8192])
    b_sm = sbuf("b_sm", [128, 2080])
    b_ct = sbuf("b_ct", [128, 1408])

    rem = nc.sbuf_bytes_remaining

    CONSTVALS = [1e-5, -0.5, 0.5, 2.0, -2.0, 4.0, 0.125, -1.0]
    for i, v in enumerate(CONSTVALS):
        nc.const_aps.register(F32, float(v), b_sm[:, 1024 + i:1024 + i + 1])

    ones_col = b_sm[:, 0:1]
    ones_row = b_sm[0:1, 2:130]
    c_bqk = b_sm[:, 130:146]
    pc = b_sm[:, 146:210]
    colsq = b_sm[:, 210:218]
    zn_s = b_sm[:, 218:226]
    e2_s = b_sm[:, 226:234]
    em_s = b_sm[:, 234:242]
    rzn_s = b_sm[:, 242:250]
    bv_rep = b_sm[:, 1040:2064]

    t_eye = sbuf("t_eye", [128, 128])

    psA = nc.alloc_psum_tensor("psA", [128, 2048], F32).ap()
    psB = nc.alloc_psum_tensor("psB", [128, 2048], F32).ap()

    P.op(VE, lambda e: e.memset(ones_col, 1.0), writes=["ones"])
    P.op(VE, lambda e: e.memset(ones_row, 1.0), writes=["ones"])
    P.op(SY, lambda e: e.dma_start(out=b_sm[:, 1024:1032], in_=consts[:]),
         writes=["consts"], dma="cc")
    P.op(SY, lambda e: e.dma_start(out=c_bqk, in_=bqk_pp[:]), writes=["bqk"], dma="cc")
    P.op(SY, lambda e: e.dma_start(out=t_eye[:], in_=eye[:]), writes=["eye"], dma="cc")
    bvr = b_xn[0:1, 2304:3328]
    P.op(SY, lambda e: e.dma_start(out=bvr, in_=bv_row[:]), writes=["bvr"], dma="cc")
    for c in range(2):
        P.op(TE, lambda e, c=c: e.matmul(psA[:, c * 512:(c + 1) * 512],
                                         ones_row[0:1, 0:128],
                                         bvr[0:1, c * 512:(c + 1) * 512],
                                         start=True, stop=True),
             reads=["ones", "bvr"], writes=[("psA", c)])
        P.op(VE, lambda e, c=c: e.tensor_copy(bv_rep[:, c * 512:(c + 1) * 512],
                                              psA[:, c * 512:(c + 1) * 512]),
             reads=[("psA", c)], writes=["bvrep"])

    def poin_consts(zt, bco, pco):
        for mt in range(MT):
            w = b_pw[:, (mt % 2) * 1024:(mt % 2) * 1024 + 1024]
            P.op(SY, lambda e, mt=mt, w=w, zt=zt: e.dma_start(
                out=w, in_=zt[mt * 128:(mt + 1) * 128, :]),
                writes=[("pw", mt % 2)], dma=f"zw{mt % 2}")
            for kt in range(KT):
                P.op(SC, lambda e, w=w, kt=kt: e.activation(
                    b_scr[:, 0:128], w[:, kt * 128:(kt + 1) * 128], AF.Square),
                    reads=[("pw", mt % 2)], writes=["sc0"])
                P.op(TE, lambda e, kt=kt: e.matmul(
                    psB[0:1, 0:128], ones_col, b_scr[:, 0:128],
                    start=(kt == 0), stop=(kt == KT - 1)),
                    reads=["sc0", "ones"], writes=[("psB", 0)])
            P.op(VE, lambda e: e.tensor_copy(b_scr[0:1, 512:640], psB[0:1, 0:128]),
                 reads=[("psB", 0)], writes=["sc1"])
            P.op(TE, lambda e, mt=mt: e.transpose(
                psA[:, 512 + mt:513 + mt], b_scr[0:1, 512:640],
                t_eye[0:1, 0:1]),
                reads=["sc1", "eye"], writes=[("psA", 1)])
        P.op(VE, lambda e: e.tensor_copy(colsq, psA[:, 512:520]),
             reads=[("psA", 1)], writes=["colsq"])
        P.op(SC, lambda e: e.activation(zn_s, colsq, AF.Ln),
             reads=["colsq"], writes=["zn"])
        P.op(SC, lambda e: e.activation(zn_s, zn_s, AF.Exp, scale=0.5),
             reads=["zn", "consts"], writes=["zn"])
        P.op(VE, lambda e: e.tensor_scalar(zn_s, zn_s, 1e-15, None, A.max),
             reads=["zn"], writes=["zn"])
        bc = c_bqk[:, bco:bco + 8]
        P.op(SC, lambda e, bc=bc: e.activation(e2_s, bc, AF.Exp, scale=2.0),
             reads=["bqk", "consts"], writes=["e2"])
        P.op(SC, lambda e, bc=bc: e.activation(em_s, bc, AF.Exp, scale=-2.0),
             reads=["bqk", "consts"], writes=["em"])
        c1 = pc[:, pco:pco + 8]
        s2r = pc[:, pco + 8:pco + 16]
        t2 = pc[:, pco + 16:pco + 24]
        nt2 = pc[:, pco + 24:pco + 32]
        P.op(VE, lambda e: e.reciprocal(rzn_s, zn_s), reads=["zn"], writes=["rzn"])
        P.op(VE, lambda e, s2r=s2r: e.tensor_sub(s2r, e2_s, em_s),
             reads=["e2", "em"], writes=["pcs"])
        P.op(VE, lambda e, s2r=s2r: e.tensor_scalar(s2r, s2r, 0.5, None, A.mult),
             reads=["pcs"], writes=["pcs"])
        P.op(VE, lambda e, c1=c1: e.tensor_add(c1, e2_s, em_s),
             reads=["e2", "em"], writes=["pcc"])
        P.op(VE, lambda e, c1=c1: e.tensor_scalar(c1, c1, 0.5, None, A.mult),
             reads=["pcc"], writes=["pcc"])
        P.op(VE, lambda e, c1=c1: e.tensor_mul(c1, c1, rzn_s),
             reads=["pcc", "rzn"], writes=["pcc"])
        P.op(VE, lambda e, t2=t2: e.tensor_scalar(t2, zn_s, 2.0, None, A.mult),
             reads=["zn"], writes=["pct"])
        P.op(VE, lambda e, nt2=nt2: e.tensor_scalar(nt2, zn_s, -2.0, None, A.mult),
             reads=["zn"], writes=["pcn"])

    poin_consts(zq_t, 0, 0)
    poin_consts(zk_t, 8, 32)

    P.barrier()

    nxin = b_scr[:, 0:512]
    nxt = b_scr[:, 512:1024]
    nxu = b_scr[:, 1024:1536]
    nxa = b_scr[:, 1536:2048]
    nxg = b_scr[:, 2048:2560]
    nxh = b_scr[:, 2560:3072]
    nxbm = b_ct[:, 0:32]
    nxsc = b_ct[:, 32:64]

    def emit_qtile(src, dst, r0, cs, W, SL=None, NM="nx_", ch="xn0", qeng=GP,
                   preloaded=False, nxt_u=None):
        nb = W // 16
        if SL is None:
            _in, _t, _u, _a, _g, _h = nxin, nxt, nxu, nxa, nxg, nxh
            _bm, _sc = nxbm, nxsc
        else:
            _in, _t, _u, _a, _g, _h, _bm, _sc = SL
        K_in, K_t, K_u, K_a, K_g, K_h, K_bm, K_sc = (
            NM + "in", NM + "t", NM + "u", NM + "a", NM + "g", NM + "h",
            NM + "bm", NM + "sc")
        def q_inload(src, dst, r0, cs, W):
            P.op(qeng, lambda e: e.dma_start(
                out=_in[:, 0:W], in_=src[r0:r0 + 128, cs:cs + W]),
                writes=[K_in], dma=ch)
        if not preloaded:
            q_inload(src, dst, r0, cs, W)
        P.op(VE, lambda e: e.transpose(_t[:, 0:W], _in[:, 0:W]),
             reads=[K_in], writes=[K_t])
        if nxt_u is not None:
            q_inload(*nxt_u)
        P.op(VE, lambda e: e.reduce_max(
            _bm[:, 0:nb], _t[:, 0:W].rearrange("p (b c) -> p b c", c=16),
            AX.X, apply_absolute_value=True),
            reads=[K_t], writes=[K_bm])
        P.op(VE, lambda e: e.tensor_scalar(_sc[:, 0:nb], _bm[:, 0:nb],
                                           1.0 / 6.0, 1e-12, A.mult, A.max),
             reads=[K_bm], writes=[K_sc])
        P.op(VE, lambda e: e.reciprocal(_bm[:, 0:nb], _sc[:, 0:nb]),
             reads=[K_sc], writes=[K_bm])
        scb = _sc[:, 0:nb].rearrange("p (b o) -> p b o", o=1) \
            .broadcast_to([128, nb, 16])
        rcb = _bm[:, 0:nb].rearrange("p (b o) -> p b o", o=1) \
            .broadcast_to([128, nb, 16])
        P.op(VE, lambda e: e.tensor_tensor(
            _u[:, 0:W].rearrange("p (b c) -> p b c", c=16),
            _t[:, 0:W].rearrange("p (b c) -> p b c", c=16), rcb, A.mult),
            reads=[K_t, K_bm], writes=[K_u])
        P.op(VE, lambda e: e.scalar_tensor_tensor(
            _a[:, 0:W], _u[:, 0:W], -1.0, _u[:, 0:W], A.mult, A.max),
            reads=[K_u], writes=[K_a])
        P.op(VE, lambda e: e.tensor_scalar(_g[:, 0:W], _a[:, 0:W],
                                           0.25, None, A.is_gt),
             reads=[K_a], writes=[K_g])
        for th in (0.75, 1.25, 1.75):
            P.op(VE, lambda e, th=th: e.scalar_tensor_tensor(
                _g[:, 0:W], _a[:, 0:W], th, _g[:, 0:W], A.is_gt, A.add),
                reads=[K_a, K_g], writes=[K_g])
        P.op(VE, lambda e: e.tensor_scalar(_h[:, 0:W], _a[:, 0:W],
                                           2.5, None, A.is_gt),
             reads=[K_a], writes=[K_h])
        P.op(VE, lambda e: e.scalar_tensor_tensor(
            _h[:, 0:W], _a[:, 0:W], 3.5, _h[:, 0:W], A.is_gt, A.add),
            reads=[K_a, K_h], writes=[K_h])
        P.op(VE, lambda e: e.scalar_tensor_tensor(
            _g[:, 0:W], _g[:, 0:W], 0.5, _h[:, 0:W], A.mult, A.add),
            reads=[K_g, K_h], writes=[K_g])
        P.op(VE, lambda e: e.tensor_scalar(_h[:, 0:W], _a[:, 0:W],
                                           5.0, None, A.is_gt),
             reads=[K_a], writes=[K_h])
        P.op(VE, lambda e: e.scalar_tensor_tensor(
            _g[:, 0:W], _h[:, 0:W], 2.0, _g[:, 0:W], A.mult, A.add),
            reads=[K_h, K_g], writes=[K_g])
        P.op(VE, lambda e: e.tensor_tensor(
            _t[:, 0:W].rearrange("p (b c) -> p b c", c=16),
            _g[:, 0:W].rearrange("p (b c) -> p b c", c=16), scb, A.mult),
            reads=[K_g, K_sc], writes=[K_t])
        P.op(VE, lambda e: e.tensor_scalar(_a[:, 0:W], _u[:, 0:W],
                                           0.0, 2.0, A.is_ge, A.mult),
             reads=[K_u], writes=[K_a])
        P.op(VE, lambda e: e.scalar_tensor_tensor(
            _g[:, 0:W], _a[:, 0:W], 1.0, _t[:, 0:W], A.subtract, A.mult),
            reads=[K_a, K_t], writes=[K_g])
        P.op(VE, lambda e: e.transpose(_u[:, 0:W], _g[:, 0:W]),
             reads=[K_g], writes=[K_u])
        P.op(GP, lambda e: e.dma_start(
            out=dst[r0:r0 + 128, cs:cs + W], in_=_u[:, 0:W]),
            reads=[K_u], dma="sp")


    _qw = []
    for _wsrc, _wdst, _R, _C in [(wg_T, wqg_d, D, II), (wu_T, wqu_d, D, II),
                                 (wd_T, wqd_d, II, D)]:
        for _rt in range(_R // 128):
            _c0 = 0
            while _c0 < _C:
                _W = min(512, _C - _c0)
                _qw.append((_wsrc, _wdst, _rt * 128, _c0, _W))
                _c0 += _W
    N_P1Q = min(12 * NCH1, len(_qw))
    _qw_p1 = _qw[:N_P1Q]
    _qw_rest = _qw[N_P1Q:]
    p1q_slots = (b_at[:, 5120:5632], b_at[:, 5632:6144], b_at[:, 6144:6656],
                 b_at[:, 6656:7168], b_at[:, 7168:7680], b_at[:, 7680:8192],
                 b_ct[:, 64:96], b_ct[:, 96:128])
    p1q_next = [0]

    def p1_qunit(n=1):
        for _ in range(n):
            i = p1q_next[0]
            if i < len(_qw_p1):
                emit_qtile(*_qw_p1[i], SL=p1q_slots, NM="pq_",
                           ch="pq0", qeng=SY, preloaded=(i > 0),
                           nxt_u=_qw_p1[i + 1]
                           if i + 1 < len(_qw_p1) else None)
                p1q_next[0] += 1

    x_t = [b_x[:, kt * 512:(kt + 1) * 512] for kt in range(KT)]
    xn_t = [b_xn[:, kt * 512:(kt + 1) * 512] for kt in range(KT)]
    q_t = [b_qk[:, mt * 512:(mt + 1) * 512] for mt in range(MT)]
    k_t = [b_qk[:, 4096 + mt * 512:4096 + (mt + 1) * 512] for mt in range(MT)]
    vt_t = [b_vt[:, i * VTW:(i + 1) * VTW] for i in range(NVT)]
    r_m = b_at[0:1, 0:512]
    r_t = b_at[0:1, 512:1024]
    r_var = b_at[0:1, 1024:1536]
    r_rstd = b_at[0:1, 1536:2048]
    r_mr = b_at[0:1, 2048:2560]
    r_lam = b_at[0:1, 2560:3072]
    r_lam1 = b_at[0:1, 3072:3584]
    r_d = b_at[0:1, 3584:4096]
    lam_rep = b_at[:, 4096:4608]
    lam1_rep = b_at[:, 4608:5120]
    s_a = b_scr[:, 0:512]
    s_b = b_scr[:, 512:1024]
    s_c = b_scr[:, 1024:1536]
    s_d2 = b_scr[:, 1536:2048]
    s_e = b_scr[:, 2048:2560]

    ssets = [
        dict(a=b_scr[:, 0:512], b=b_scr[:, 512:1024], c=b_scr[:, 1024:1536],
             d=b_scr[:, 1536:2048], e=b_scr[:, 2048:2560]),
        dict(a=b_scr[:, 2560:3072], b=b_x[:, 4096:4608],
             c=b_x[:, 4608:5120], d=b_x[:, 5120:5632],
             e=b_x[:, 5632:6144]),
    ]

    def proj_poincare(proj, zt, pco, spill, tcs, tail_hook=None):
        qt = q_t if proj == "q" else k_t

        def emit_mms(mt):
            wb = b_pw[:, (mt % 2) * 1024:(mt % 2) * 1024 + 1024]
            P.op(SY, lambda e, mt=mt, wb=wb, zt=zt: e.dma_start(
                out=wb, in_=zt[mt * 128:(mt + 1) * 128, :]),
                writes=[("pw", mt % 2)], dma=f"zw{mt % 2}")
            pv = psB[:, 1024 + (mt % 2) * 512:1536 + (mt % 2) * 512]
            pvk = ("psB", 2 + mt % 2)
            for kt in range(KT):
                P.op(TE, lambda e, wb=wb, kt=kt, pv=pv: e.matmul(
                    pv, wb[:, kt * 128:(kt + 1) * 128], xn_t[kt],
                    start=(kt == 0), stop=(kt == KT - 1)),
                    reads=[("pw", mt % 2), ("xnt", kt)], writes=[pvk])
            return pv, pvk

        def chain_stages(mt, si, pv, pvk):
            ss = ssets[si]
            sfx = str(si)
            c1 = pc[:, pco + mt:pco + mt + 1]
            s2r = pc[:, pco + 8 + mt:pco + 8 + mt + 1]
            t2 = pc[:, pco + 16 + mt:pco + 16 + mt + 1]
            nt2 = pc[:, pco + 24 + mt:pco + 24 + mt + 1]
            sa, sb2, sc3, sd, se = ss["a"], ss["b"], ss["c"], ss["d"], ss["e"]
            yield lambda: P.op(VE, lambda e: e.scalar_tensor_tensor(
                sa, pv, c1, lam_rep, A.mult, A.mult),
                reads=[pvk, "lamr", "pcc"], writes=["sa" + sfx])
            yield lambda: P.op(VE, lambda e: e.scalar_tensor_tensor(
                sb2, lam1_rep, s2r, sa, A.mult, A.subtract),
                reads=["lam1r", "sa" + sfx, "pcs"], writes=["sb" + sfx])
            yield lambda: P.op(SC, lambda e: e.activation(sc3, sb2, AF.Square),
                               reads=["sb" + sfx], writes=["sc" + sfx])
            yield lambda: P.op(SC, lambda e: e.activation(sc3, sc3, AF.Ln,
                                                          bias=1.0),
                               reads=["sc" + sfx], writes=["sc" + sfx])
            yield lambda: P.op(SC, lambda e: e.activation(sc3, sc3, AF.Exp,
                                                          scale=0.5),
                               reads=["sc" + sfx, "consts"], writes=["sc" + sfx])
            yield lambda: P.op(VE, lambda e: e.tensor_add(sc3, sb2, sc3),
                               reads=["sb" + sfx, "sc" + sfx], writes=["sc" + sfx])
            yield lambda: P.op(SC, lambda e: e.activation(sc3, sc3, AF.Ln),
                               reads=["sc" + sfx], writes=["sc" + sfx])
            yield lambda: P.op(SC, lambda e: e.activation(sd, sc3, AF.Exp,
                                                          scale=t2),
                               reads=["sc" + sfx, "pct"], writes=["sd" + sfx])
            yield lambda: P.op(SC, lambda e: e.activation(se, sc3, AF.Exp,
                                                          scale=nt2),
                               reads=["sc" + sfx, "pcn"], writes=["se" + sfx])
            yield lambda: P.op(VE, lambda e: e.tensor_sub(qt[mt], sd, se),
                               reads=["sd" + sfx, "se" + sfx],
                               writes=[("qt", (proj, mt))])
            yield lambda: P.op(SC, lambda e: e.activation(sa, qt[mt],
                                                          AF.Square),
                               reads=[("qt", (proj, mt))], writes=["sa" + sfx])
            yield lambda: P.op(TE, lambda e, mt=mt: e.matmul(
                psB[0:1, 0:512], ones_col, sa,
                start=(mt == 0), stop=(mt == MT - 1)),
                reads=["sa" + sfx, "ones"], writes=[("psB", 0)])

        pending_tail = []
        for b0 in range(0, MT, 2):
            pv0, pvk0 = emit_mms(b0)
            pv1, pvk1 = emit_mms(b0 + 1)
            if b0 == 0 and tail_hook is not None:
                tail_hook()
            for t0, t1 in pending_tail:
                t0()
                t1()
            pending_tail = []
            g0 = list(chain_stages(b0, 0, pv0, pvk0))
            g1 = list(chain_stages(b0 + 1, 1, pv1, pvk1))
            for st0, st1 in zip(g0[:-2], g1[:-2]):
                st0()
                st1()
            pending_tail = [(g0[-2], g1[-2]), (g0[-1], g1[-1])]
        for t0, t1 in pending_tail:
            t0()
            t1()
        rd_ps = psA[:, 0:512] if proj == "q" else psB[:, 0:512]
        rd_key = ("psA", 0) if proj == "q" else ("psB", 0)

        def tail():
            P.op(SC, lambda e: e.activation(r_d, psB[0:1, 0:512], AF.Ln, bias=4.0),
                 reads=[("psB", 0), "consts"], writes=["r_d"])
            P.op(SC, lambda e: e.activation(r_d, r_d, AF.Exp, scale=0.5),
                 reads=["r_d", "consts"], writes=["r_d"])
            P.op(VE, lambda e: e.tensor_scalar(r_d, r_d, 2.0, None, A.add),
                 reads=["r_d"], writes=["r_d"])
            P.op(VE, lambda e: e.reciprocal(r_d, r_d), reads=["r_d"], writes=["r_d"])
            P.op(TE, lambda e: e.matmul(rd_ps, ones_row, r_d,
                                        start=True, stop=True),
                 reads=["r_d", "ones"], writes=[rd_key])
            for mt in range(MT):
                P.op(VE, lambda e, qt=qt, mt=mt: e.tensor_tensor(
                    qt[mt], qt[mt], rd_ps, A.mult),
                    reads=[("qt", (proj, mt)), rd_key],
                    writes=[("qt", (proj, mt))])
                P.op(GP, lambda e, qt=qt, mt=mt, spill=spill, tcs=tcs: e.dma_start(
                    out=spill[mt * 128:(mt + 1) * 128, tcs], in_=qt[mt]),
                    reads=[("qt", (proj, mt))], dma="sp")
        return tail


    scr_sq = b_x[:, 6144:6656]

    def p1_xload(tc):
        tcs = slice(tc * 512, (tc + 1) * 512)
        for kt in range(KT):
            P.op(SY, lambda e, kt=kt, tcs=tcs: e.dma_start(
                out=x_t[kt], in_=x_fm[kt * 128:(kt + 1) * 128, tcs]),
                writes=[("xt", kt)], dma="xl")

    def p1_stats():
        for kt in range(KT):
            P.op(SC, lambda e, kt=kt: e.activation(scr_sq, x_t[kt], AF.Square),
                 reads=[("xt", kt)], writes=["sqx"])
            P.op(TE, lambda e, kt=kt: e.matmul(psB[0:1, 0:512], ones_col, scr_sq,
                                               start=(kt == 0), stop=(kt == KT - 1)),
                 reads=["sqx", "ones"], writes=[("psB", 0)])
            P.op(TE, lambda e, kt=kt: e.matmul(psB[0:1, 512:1024], ones_col,
                                               x_t[kt],
                                               start=(kt == 0), stop=(kt == KT - 1)),
                 reads=[("xt", kt), "ones"], writes=[("psB", 1)])

    def p1_rows():
        P.op(VE, lambda e: e.tensor_scalar(r_m, psB[0:1, 512:1024], 1.0 / D,
                                           None, A.mult),
             reads=[("psB", 1)], writes=["r_m"])
        P.op(VE, lambda e: e.tensor_mul(r_t, r_m, r_m), reads=["r_m"], writes=["r_t"])
        P.op(VE, lambda e: e.scalar_tensor_tensor(r_var, psB[0:1, 0:512],
                                                  1.0 / D, r_t, A.mult, A.subtract),
             reads=[("psB", 0), "r_t"], writes=["r_var"])
        P.op(SC, lambda e: e.activation(r_rstd, r_var, AF.Ln, bias=1e-5),
             reads=["r_var", "consts"], writes=["r_rstd"])
        P.op(SC, lambda e: e.activation(r_rstd, r_rstd, AF.Exp, scale=-0.5),
             reads=["r_rstd", "consts"], writes=["r_rstd"])
        P.op(VE, lambda e: e.tensor_mul(r_mr, r_m, r_rstd),
             reads=["r_m", "r_rstd"], writes=["r_mr"])
        P.op(VE, lambda e: e.tensor_scalar(r_t, r_var, 1e-5, None, A.add),
             reads=["r_var"], writes=["r_t"])
        P.op(VE, lambda e: e.reciprocal(r_t, r_t), reads=["r_t"], writes=["r_t"])
        P.op(VE, lambda e: e.tensor_mul(r_t, r_var, r_t),
             reads=["r_var", "r_t"], writes=["r_t"])
        P.op(VE, lambda e: e.tensor_scalar(r_t, r_t, -float(D), 1.0, A.mult, A.add),
             reads=["r_t"], writes=["r_t"])
        P.op(VE, lambda e: e.reciprocal(r_t, r_t), reads=["r_t"], writes=["r_t"])
        P.op(VE, lambda e: e.tensor_scalar(r_lam, r_t, 2.0, None, A.mult),
             reads=["r_t"], writes=["r_lam"])
        P.op(VE, lambda e: e.tensor_scalar(r_lam1, r_lam, -1.0, None, A.add),
             reads=["r_lam"], writes=["r_lam1"])

    def p1_reps_xn():
        P.op(TE, lambda e: e.matmul(psA[:, 0:512], ones_row, r_rstd,
                                    start=True, stop=True),
             reads=["r_rstd", "ones"], writes=[("psA", 0)])
        P.op(TE, lambda e: e.matmul(psA[:, 512:1024], ones_row, r_mr,
                                    start=True, stop=True),
             reads=["r_mr", "ones"], writes=[("psA", 1)])
        P.op(TE, lambda e: e.matmul(psA[:, 1024:1536], ones_row, r_lam,
                                    start=True, stop=True),
             reads=["r_lam", "ones"], writes=[("psA", 2)])
        P.op(TE, lambda e: e.matmul(psA[:, 1536:2048], ones_row, r_lam1,
                                    start=True, stop=True),
             reads=["r_lam1", "ones"], writes=[("psA", 3)])
        P.op(VE, lambda e: e.tensor_copy(lam_rep, psA[:, 1024:1536]),
             reads=[("psA", 2)], writes=["lamr"])
        P.op(VE, lambda e: e.tensor_copy(lam1_rep, psA[:, 1536:2048]),
             reads=[("psA", 3)], writes=["lam1r"])
        for kt in range(KT):
            P.op(VE, lambda e, kt=kt: e.tensor_tensor(xn_t[kt], x_t[kt],
                                                      psA[:, 0:512], A.mult),
                 reads=[("xt", kt), ("psA", 0)], writes=[("xnt", kt)])
            P.op(VE, lambda e, kt=kt: e.tensor_tensor(xn_t[kt], xn_t[kt],
                                                      psA[:, 512:1024], A.subtract),
                 reads=[("xnt", kt), ("psA", 1)], writes=[("xnt", kt)])

    def p1_v_wave(tc, wave, hook=None):
        for kt in range(KT):
            if kt == 2 and hook is not None:
                hook()
            wb = b_pw[:, (kt % 2) * 1024:(kt % 2) * 1024 + 1024]
            P.op(SY, lambda e, kt=kt, wb=wb: e.dma_start(
                out=wb, in_=wv_T[kt * 128:(kt + 1) * 128, :]),
                writes=[("pw", kt % 2)], dma=f"zw{kt % 2}")
            for tt in (wave * 2, wave * 2 + 1):
                for nh2 in range(2):
                    bank = (tt % 2) * 2 + nh2
                    pvv = psA[:, bank * 512:(bank + 1) * 512]
                    P.op(TE, lambda e, wb=wb, kt=kt, tt=tt, nh2=nh2, pvv=pvv:
                         e.matmul(
                             pvv, xn_t[kt][:, tt * 128:(tt + 1) * 128],
                             wb[:, nh2 * 512:(nh2 + 1) * 512],
                             start=(kt == 0), stop=(kt == KT - 1)),
                         reads=[("pw", kt % 2), ("xnt", kt)],
                         writes=[("psA", bank)])
        for tt in (wave * 2, wave * 2 + 1):
            vi = tc * 4 + tt
            P.op(VE, lambda e, vi=vi: e.memset(
                vt_t[vi].rearrange("p (h w) -> p h w", w=65)[:, :, 64:65], 1.0),
                writes=[("vt", vi)])
            for nh2 in range(2):
                bank = (tt % 2) * 2 + nh2
                pvv = psA[:, bank * 512:(bank + 1) * 512]
                P.op(VE, lambda e, vi=vi, nh2=nh2, pvv=pvv: e.scalar_tensor_tensor(
                    vt_t[vi].rearrange("p (h w) -> p h w", w=65)
                    [:, nh2 * 8:(nh2 + 1) * 8, 0:64],
                    pvv.rearrange("p (h d) -> p h d", d=64), 1.0,
                    bv_rep[:, nh2 * 512:(nh2 + 1) * 512]
                    .rearrange("p (h d) -> p h d", d=64),
                    A.mult, A.add),
                    reads=[("psA", bank), "bvrep"], writes=[("vt", vi)])

    p1_xload(0)
    p1_stats()
    p1_rows()
    for tc in range(NCH1):
        tcs = slice(tc * 512, (tc + 1) * 512)
        p1_reps_xn()
        tq_tail = proj_poincare("q", zq_t, 0, q_sp, tcs)
        p1_qunit(3)
        tk_tail = proj_poincare("k", zk_t, 32, k_sp, tcs, tail_hook=tq_tail)
        if tc + 1 < NCH1:
            p1_xload(tc + 1)
        p1_qunit(3)
        p1_v_wave(tc, 0, hook=tk_tail)
        if tc + 1 < NCH1:
            p1_stats()
        p1_qunit(3)
        p1_v_wave(tc, 1)
        p1_qunit(3)
        if tc + 1 < NCH1:
            p1_rows()
        for kt in range(KT):
            P.op(GP, lambda e, kt=kt, tcs=tcs: e.dma_start(
                out=xn_sp[kt * 128:(kt + 1) * 128, tcs], in_=xn_t[kt]),
                reads=[("xnt", kt)], dma="sp")

    P.barrier()

    SC2 = 512
    NC2 = S // SC2
    K2buf = [b_qk[:, 0:S], b_qk[:, S:2 * S]]
    Q2buf = [b_qk[:, 2 * S:2 * S + 512], b_qk[:, 2 * S + 512:2 * S + 1024]]
    eslots = []
    for buf, c0, c1 in [(b_x, 0, 7040), (b_qk, 2 * S + 1024, 8192),
                        (b_pw, 0, 2048), (b_at, 4096, 8192),
                        (b_xn, 2560, 4096)]:
        c = c0
        while c + 512 <= c1:
            eslots.append(buf[:, c:c + 512])
            c += 512
    assert len(eslots) >= 2 * NVT, (len(eslots), NVT)
    Et = [[eslots[p * NVT + tk] for tk in range(NVT)] for p in range(2)]
    at_t = [b_at[:, mt * 512:(mt + 1) * 512] for mt in range(MT)]
    rs_t = b_xn[0:16, 0:512]
    rs_scr = [b_xn[64:65, 1536:2048], b_xn[64:65, 2048:2560]]

    qunits = list(_qw_rest)
    for tc in range(NCH1):
        for kt in range(KT):
            qunits.append((xn_sp, xq_sp, kt * 128, tc * 512, 512))

    per_iter = (len(qunits) + NC2 * 8 - 1) // (NC2 * 8)
    q_next = [0]

    def emit_unit():
        i = q_next[0]
        if i < len(qunits):
            emit_qtile(*qunits[i], preloaded=(i > 0),
                       nxt_u=qunits[i + 1] if i + 1 < len(qunits) else None)
            q_next[0] += 1
            return True
        return False

    KP = max(1, NVT // 4)

    def load_pair(ch, hp):
        chs2 = slice(ch * SC2, (ch + 1) * SC2)
        P.op(SY, lambda e, hp=hp, chs2=chs2: e.dma_start(
            out=Q2buf[hp % 2], in_=q_sp[hp * 128:(hp + 1) * 128, chs2]),
            writes=[("qb", hp % 2)], dma=f"q{hp % 2}")
        for pi in range(NVT // KP):
            csl = slice(pi * KP * 128, (pi + 1) * KP * 128)
            P.op(SY, lambda e, hp=hp, csl=csl: e.dma_start(
                out=K2buf[hp % 2][:, csl], in_=k_sp[hp * 128:(hp + 1) * 128, csl]),
                writes=[("kb", (hp % 2, pi))], dma=f"k{hp % 2}p{pi}")

    load_pair(0, 0)
    for ch in range(NC2):
        chs = slice(ch * SC2, (ch + 1) * SC2)
        for hp in range(8):
            if hp < 7:
                load_pair(ch, hp + 1)
            elif ch + 1 < NC2:
                load_pair(ch + 1, 0)
            for _ in range(per_iter):
                emit_unit()
            for tk in range(NVT):
                for par in range(2):
                    bank = par * 2 + (tk % 2)
                    P.op(TE, lambda e, hp=hp, tk=tk, par=par, bank=bank: e.matmul(
                        psA[:, bank * 512:(bank + 1) * 512],
                        K2buf[hp % 2][par * 64:(par + 1) * 64,
                                      tk * 128:(tk + 1) * 128],
                        Q2buf[hp % 2][par * 64:(par + 1) * 64, :],
                        start=True, stop=True),
                        reads=[("kb", (hp % 2, tk // KP)), ("qb", hp % 2)],
                        writes=[("psA", bank)])
                for par in range(2):
                    bank = par * 2 + (tk % 2)
                    P.op(SC, lambda e, tk=tk, par=par, bank=bank: e.activation(
                        Et[par][tk], psA[:, bank * 512:(bank + 1) * 512],
                        AF.Exp, scale=0.125),
                        reads=[("psA", bank), "consts"],
                        writes=[("eb", (par, tk))])
            for tk in range(NVT):
                for par in range(2):
                    P.op(TE, lambda e, hp=hp, tk=tk, par=par: e.matmul(
                        psB[0:65, par * 512:(par + 1) * 512],
                        vt_t[tk][:, (2 * hp + par) * 65:(2 * hp + par) * 65 + 65],
                        Et[par][tk],
                        start=(tk == 0), stop=(tk == NVT - 1)),
                        reads=[("eb", (par, tk)), ("vt", tk)],
                        writes=[("psB", par)])
            for par in range(2):
                P.op(SC, lambda e, hp=hp, par=par: e.activation(
                    at_t[hp][par * 64:(par + 1) * 64, :],
                    psB[0:64, par * 512:(par + 1) * 512], AF.Copy),
                    reads=[("psB", par)], writes=[("at", hp)])
                P.op(SC, lambda e, par=par: e.activation(
                    rs_scr[par], psB[64:65, par * 512:(par + 1) * 512],
                    AF.Copy),
                    reads=[("psB", par)], writes=[("rss", par)])
                P.op(SY, lambda e, hp=hp, par=par: e.dma_start(
                    out=rs_t[2 * hp + par:2 * hp + par + 1, :],
                    in_=rs_scr[par]),
                    reads=[("rss", par)], writes=[("rs", par)],
                    dma=f"rs{par}")
            P.op(SY, lambda e, hp=hp, chs=chs: e.dma_start(
                out=at_sp[hp * 128:(hp + 1) * 128, chs], in_=at_t[hp]),
                reads=[("at", hp)], dma="at")
        P.op(SY, lambda e, chs=chs: e.dma_start(
            out=rs_dr[:, chs], in_=rs_t), reads=["rs"], dma="rsd")
    while emit_unit():
        pass

    P.barrier()

    xq_t = [b_vt[:, kt * S:(kt + 1) * S] for kt in range(KT)]
    for kt in range(KT):
        P.op(SY, lambda e, kt=kt: e.dma_start(
            out=xq_t[kt], in_=xq_sp[kt * 128:(kt + 1) * 128, :]),
            writes=[("xqt", kt)], dma="xl")
    sg_t = b_xn[:, 0:S]
    h_t = b_xn[:, 2048:2048 + S] if S <= 2048 else None
    wgbuf = [b_qk[:, 0:1024], b_qk[:, 1024:2048]]
    wubuf = [b_qk[:, 2048:3072], b_qk[:, 3072:4096]]
    hq_t = b_qk[:, 4096:4096 + S]
    hbm = b_ct[:, 0:64]
    hsc2 = b_ct[:, 64:128]
    for m in range(IT):
        P.op(SY, lambda e, m=m: e.dma_start(
            out=wgbuf[m % 2].rearrange("p (kt c) -> p kt c", c=128),
            in_=wqg_d.rearrange("(kt p) o -> p kt o", p=128)
            [:, :, m * 128:(m + 1) * 128]),
            writes=[("wgb", m % 2)], dma=f"wf{m % 2}")
        P.op(SY, lambda e, m=m: e.dma_start(
            out=wubuf[m % 2].rearrange("p (kt c) -> p kt c", c=128),
            in_=wqu_d.rearrange("(kt p) o -> p kt o", p=128)
            [:, :, m * 128:(m + 1) * 128]),
            writes=[("wub", m % 2)], dma=f"wf{m % 2}")
        for kt in range(KT):
            for n in range(NTQ):
                P.op(TE, lambda e, m=m, kt=kt, n=n: e.matmul(
                    psA[:, n * 512:(n + 1) * 512],
                    wgbuf[m % 2][:, kt * 128:(kt + 1) * 128],
                    xq_t[kt][:, n * 512:(n + 1) * 512],
                    start=(kt == 0), stop=(kt == KT - 1)),
                    reads=[("wgb", m % 2), ("xqt", kt)], writes=[("psA", n)])
                P.op(TE, lambda e, m=m, kt=kt, n=n: e.matmul(
                    psB[:, n * 512:(n + 1) * 512],
                    wubuf[m % 2][:, kt * 128:(kt + 1) * 128],
                    xq_t[kt][:, n * 512:(n + 1) * 512],
                    start=(kt == 0), stop=(kt == KT - 1)),
                    reads=[("wub", m % 2), ("xqt", kt)], writes=[("psB", n)])
        P.op(SC, lambda e: e.activation(sg_t, psA[:, 0:S], AF.Silu),
             reads=[("psA", n) for n in range(NTQ)], writes=["sg"])
        P.op(VE, lambda e: e.scalar_tensor_tensor(h_t, psB[:, 0:S], 1.0, sg_t,
                                                  A.mult, A.mult),
             reads=[("psB", n) for n in range(NTQ)] + ["sg"], writes=["h"])
        for hf in range(S // 1024 if S >= 1024 else 1):
            HW = min(1024, S)
            hsl = h_t[:, hf * HW:(hf + 1) * HW]
            t32 = b_scr[:, 0:HW]
            u = b_scr[:, 1024:1024 + HW]
            aa = b_scr[:, 2048:2048 + HW]
            g = b_at[:, 0:HW]
            h2 = b_at[:, 1024:1024 + HW]
            nb = HW // 16
            P.op(VE, lambda e, hsl=hsl, t32=t32: e.transpose(t32, hsl),
                 reads=["h"], writes=["hin"])
            P.op(VE, lambda e, t32=t32, nb=nb: e.reduce_max(
                hbm[:, 0:nb], t32.rearrange("p (b c) -> p b c", c=16), AX.X,
                apply_absolute_value=True),
                reads=["hin"], writes=["hbm"])
            P.op(VE, lambda e, nb=nb: e.tensor_scalar(
                hsc2[:, 0:nb], hbm[:, 0:nb], 1.0 / 6.0, 1e-12, A.mult, A.max),
                reads=["hbm"], writes=["hsc"])
            P.op(VE, lambda e, nb=nb: e.reciprocal(hbm[:, 0:nb], hsc2[:, 0:nb]),
                 reads=["hsc"], writes=["hbm"])
            scb = hsc2[:, 0:nb].rearrange("p (b o) -> p b o", o=1) \
                .broadcast_to([128, nb, 16])
            rcb = hbm[:, 0:nb].rearrange("p (b o) -> p b o", o=1) \
                .broadcast_to([128, nb, 16])
            P.op(VE, lambda e, u=u, t32=t32, rcb=rcb: e.tensor_tensor(
                u.rearrange("p (b c) -> p b c", c=16),
                t32.rearrange("p (b c) -> p b c", c=16), rcb, A.mult),
                reads=["hin", "hbm"], writes=["hu"])
            P.op(VE, lambda e, u=u, aa=aa: e.scalar_tensor_tensor(
                aa, u, -1.0, u, A.mult, A.max), reads=["hu"], writes=["ha"])
            P.op(VE, lambda e, aa=aa, g=g: e.tensor_scalar(
                g, aa, 0.25, None, A.is_gt), reads=["ha"], writes=["hg"])
            for th in (0.75, 1.25, 1.75):
                P.op(VE, lambda e, aa=aa, g=g, th=th: e.scalar_tensor_tensor(
                    g, aa, th, g, A.is_gt, A.add),
                    reads=["ha", "hg"], writes=["hg"])
            P.op(VE, lambda e, aa=aa, h2=h2: e.tensor_scalar(
                h2, aa, 2.5, None, A.is_gt), reads=["ha"], writes=["hh"])
            P.op(VE, lambda e, aa=aa, h2=h2: e.scalar_tensor_tensor(
                h2, aa, 3.5, h2, A.is_gt, A.add), reads=["ha", "hh"], writes=["hh"])
            P.op(VE, lambda e, g=g, h2=h2: e.scalar_tensor_tensor(
                g, g, 0.5, h2, A.mult, A.add), reads=["hg", "hh"], writes=["hg"])
            P.op(VE, lambda e, aa=aa, h2=h2: e.tensor_scalar(
                h2, aa, 5.0, None, A.is_gt), reads=["ha"], writes=["hh"])
            P.op(VE, lambda e, g=g, h2=h2: e.scalar_tensor_tensor(
                g, h2, 2.0, g, A.mult, A.add), reads=["hh", "hg"], writes=["hg"])
            P.op(VE, lambda e, t32=t32, g=g, scb=scb: e.tensor_tensor(
                t32.rearrange("p (b c) -> p b c", c=16),
                g.rearrange("p (b c) -> p b c", c=16), scb, A.mult),
                reads=["hg", "hsc"], writes=["hin"])
            P.op(SC, lambda e, u=u, aa=aa: e.sign(aa, u),
                 reads=["hu"], writes=["ha"])
            P.op(VE, lambda e, g=g, aa=aa, t32=t32: e.tensor_mul(g, t32, aa),
                 reads=["hin", "ha", "hg"], writes=["hg"])
            P.op(VE, lambda e, g=g, hf=hf, HW=HW: e.transpose(
                hq_t[:, hf * HW:(hf + 1) * HW], g), reads=["hg"], writes=["hqo"])
        P.op(GP, lambda e, m=m: e.dma_start(
            out=hq_sp[m * 128:(m + 1) * 128, :], in_=hq_t),
            reads=["hqo"], dma="sp")

    P.barrier()

    hqbuf = [b_vt[:, i * 512:(i + 1) * 512] for i in range(IT)]
    wdbuf = [b_qk[:, 0:1024], b_qk[:, 1024:2048]]
    att_s = [b_scr[:, 0:512], b_scr[:, 1536:2048]]
    x_s = [b_scr[:, 512:1024], b_scr[:, 2048:2560]]
    ob = b_scr[:, 1024:1536]
    def c2_hq(n, kt):
        ns2 = slice(n * 512, (n + 1) * 512)
        P.op(SY, lambda e, kt=kt, ns2=ns2: e.dma_start(
            out=hqbuf[kt], in_=hq_sp[kt * 128:(kt + 1) * 128, ns2]),
            writes=[("hqb", kt)], dma=f"hq{kt % 4}")

    def c2_wd(kt):
        P.op(SY, lambda e, kt=kt: e.dma_start(
            out=wdbuf[kt % 2], in_=wqd_d[kt * 128:(kt + 1) * 128, :]),
            writes=[("wdb", kt % 2)], dma=f"wf{kt % 2}")

    rs_rep = [b_at[:, 0:512], b_at[:, 512:1024]]
    obs = [ob, b_scr[:, 2560:3072]]

    def c2_rb(n, m):
        ns2 = slice(n * 512, (n + 1) * 512)
        for par in range(2):
            P.op(SY, lambda e, m=m, par=par, ns2=ns2: e.dma_start(
                out=rs_rep[m % 2][par * 64:(par + 1) * 64, :],
                in_=rs_dr[2 * m + par:2 * m + par + 1,
                          ns2].broadcast_to([64, 512])),
                writes=[("rsr", m % 2)], dma=f"rb{m % 2}")

    c2_hq(0, 0)
    c2_hq(0, 1)
    c2_wd(0)
    c2_rb(0, 0)
    for n in range(NTQ):
        ns = slice(n * 512, (n + 1) * 512)
        for kt in range(IT):
            if kt + 2 < IT:
                c2_hq(n, kt + 2)
            elif n + 1 < NTQ:
                c2_hq(n + 1, kt + 2 - IT)
            if kt + 1 < IT:
                c2_wd(kt + 1)
            elif n + 1 < NTQ:
                c2_wd(0)
            for m in range(MT):
                pp = psA if m < 4 else psB
                pk = ("psA" if m < 4 else "psB", m % 4)
                P.op(TE, lambda e, kt=kt, m=m, pp=pp: e.matmul(
                    pp[:, (m % 4) * 512:(m % 4) * 512 + 512],
                    wdbuf[kt % 2][:, m * 128:(m + 1) * 128], hqbuf[kt],
                    start=(kt == 0), stop=(kt == IT - 1)),
                    reads=[("wdb", kt % 2), ("hqb", kt)], writes=[pk])
        for m in range(MT):
            pp = psA if m < 4 else psB
            pk = ("psA" if m < 4 else "psB", m % 4)
            P.op(SY, lambda e, m=m, ns=ns: e.dma_start(
                out=att_s[m % 2], in_=at_sp[m * 128:(m + 1) * 128, ns]),
                writes=[("atts", m % 2)], dma=f"as{m % 2}")
            P.op(SY, lambda e, m=m, ns=ns: e.dma_start(
                out=x_s[m % 2], in_=x_fm[m * 128:(m + 1) * 128, ns]),
                writes=[("xs", m % 2)], dma=f"xs{m % 2}")
            if m + 1 < MT:
                c2_rb(n, m + 1)
            elif n + 1 < NTQ:
                c2_rb(n + 1, 0)
            P.op(VE, lambda e, m=m: e.reciprocal(rs_rep[m % 2],
                                                 rs_rep[m % 2]),
                 reads=[("rsr", m % 2)], writes=[("rsr", m % 2)])
            P.op(VE, lambda e, m=m: e.tensor_tensor(
                obs[m % 2], att_s[m % 2], rs_rep[m % 2], A.mult),
                reads=[("atts", m % 2), ("rsr", m % 2)], writes=[("ob", m % 2)])
            P.op(VE, lambda e, m=m, pp=pp: e.scalar_tensor_tensor(
                obs[m % 2], pp[:, (m % 4) * 512:(m % 4) * 512 + 512], 1.0,
                obs[m % 2], A.mult, A.add),
                reads=[pk, ("ob", m % 2)], writes=[("ob", m % 2)])
            P.op(VE, lambda e, m=m: e.tensor_add(obs[m % 2], obs[m % 2],
                                                 x_s[m % 2]),
                 reads=[("ob", m % 2), ("xs", m % 2)], writes=[("ob", m % 2)])
            P.op(GP, lambda e, m=m, ns=ns: e.dma_start(
                out=out_fm[m * 128:(m + 1) * 128, ns], in_=obs[m % 2]),
                reads=[("ob", m % 2)], dma="out")

    P.final_wait(SY)

    with contextlib.ExitStack() as st:
        sems = {}
        for i, name in enumerate(sorted(P.cnt)):
            sems[name] = st.enter_context(nc.semaphore(f"s{i}_{name[:8]}"))
        blk = st.enter_context(nc.Block())
        P.emit(blk, sems)

    return nc


def _layout_inputs(ins, S=2048):
    x = np.ascontiguousarray(ins["L0_attn_x"])
    z_q = np.ascontiguousarray(ins["L0_attn_z_q"])
    z_k = np.ascontiguousarray(ins["L0_attn_z_k"])
    w_v = np.ascontiguousarray(ins["L0_attn_w_v"])
    b_q = np.ascontiguousarray(ins["L0_attn_b_q"])
    b_k = np.ascontiguousarray(ins["L0_attn_b_k"])
    b_v = np.ascontiguousarray(ins["L0_attn_b_v"])
    w_g = np.ascontiguousarray(ins["L0_ffn_w_gate"])
    w_u = np.ascontiguousarray(ins["L0_ffn_w_up"])
    w_d = np.ascontiguousarray(ins["L0_ffn_w_down"])

    def tile_mtkt(z):
        return np.ascontiguousarray(
            z.reshape(KT, 128, MT, 128).transpose(2, 1, 0, 3).reshape(D, D))

    shared = {
        "zq_t": tile_mtkt(z_q),
        "zk_t": tile_mtkt(z_k),
        "wv_T": np.ascontiguousarray(w_v.T),
        "wg_T": np.ascontiguousarray(w_g.T),
        "wu_T": np.ascontiguousarray(w_u.T),
        "wd_T": np.ascontiguousarray(w_d.T),
        "bqk_pp": np.ascontiguousarray(
            np.concatenate([b_q.reshape(8, 128).T, b_k.reshape(8, 128).T],
                           axis=1)).astype(np.float32),
        "bv_row": np.ascontiguousarray(b_v.reshape(1, D)),
        "sel64": np.ascontiguousarray(
            (np.arange(16)[:, None] == (np.arange(D)[None, :] // 64))
            .astype(np.float32)),
        "eye": np.eye(128, dtype=np.float32),
        "consts": np.tile(np.array([[1e-5, -0.5, 0.5, 2.0, -2.0, 4.0, 0.125,
                                     -1.0]], dtype=np.float32), (128, 1)),
    }
    in_maps = []
    for b in range(x.shape[0]):
        m = dict(shared)
        m["x_fm"] = np.ascontiguousarray(x[b, :S, :].T)
        in_maps.append(m)
    return in_maps


_CACHE = {}


def kernel(**inputs):
    if "nc" not in _CACHE:
        _CACHE["nc"] = build()
    nc = _CACHE["nc"]
    in_maps = _layout_inputs(inputs)
    res = run_bass_kernel_spmd(nc, in_maps, core_ids=list(range(NCORE)))
    out = np.stack([np.asarray(r["out_fm"]).T for r in res.results], axis=0)
    return out.astype(np.float32)


# version 24 (final)
# speedup vs baseline: 1.1561x; 1.0808x over previous
import contextlib
import numpy as np
import concourse.bass as bass
import concourse.mybir as mybir
from concourse.bass_utils import run_bass_kernel_spmd

F32 = mybir.dt.float32
A = mybir.AluOpType
AF = mybir.ActivationFunctionType
AX = mybir.AxisListType

D = 1024
NH = 16
DK = 64
II = 2816
NCORE = 8
KT = 8
MT = 8
IT = 22
VTW = NH * 65

SY, TE, VE, SC, GP = "sync", "tensor", "vector", "scalar", "gpsimd"


class Prog:

    ENGINES = (SY, TE, VE, SC, GP)

    def __init__(self):
        self.ops = {e: [] for e in self.ENGINES}
        self.cnt = {e: 0 for e in self.ENGINES}
        self.state = {}
        self.regions = {}

    def _rdeps(self, t, r):
        deps = []
        if r is None:
            for reg in self.regions.get(t, ()):
                w = self.state.get((t, reg), {}).get("w")
                if w:
                    deps.append(w)
        else:
            for reg in (r, None):
                w = self.state.get((t, reg), {}).get("w")
                if w:
                    deps.append(w)
        return deps

    def _wdeps(self, t, r):
        deps = self._rdeps(t, r)
        if r is None:
            for reg in self.regions.get(t, ()):
                deps.extend(self.state.get((t, reg), {}).get("r", ()))
        else:
            for reg in (r, None):
                deps.extend(self.state.get((t, reg), {}).get("r", ()))
        return deps

    def op(self, eng, fn, reads=(), writes=(), dma=None):
        norm = lambda x: x if isinstance(x, tuple) else (x, None)
        reads = [norm(x) for x in reads]
        writes = [norm(x) for x in writes]
        deps = []
        for t, r in reads:
            deps.extend(self._rdeps(t, r))
        for t, r in writes:
            deps.extend(self._wdeps(t, r))
        waits = {}
        for s, v in deps:
            waits[s] = max(waits.get(s, 0), v)
        for s in list(waits):
            if s not in self.ENGINES:
                waits[s] = self.cnt.get(s, 0)
        sem = dma if dma else eng
        inc = 16 if dma else 1
        self.cnt[sem] = self.cnt.get(sem, 0) + inc
        my = (sem, self.cnt[sem])
        self.ops[eng].append((waits, fn, sem, inc))
        for t, r in reads:
            self.regions.setdefault(t, set()).add(r)
            self.state.setdefault((t, r), {"w": None, "r": []})["r"].append(my)
        for t, r in writes:
            self.regions.setdefault(t, set()).add(r)
            ent = self.state.setdefault((t, r), {"w": None, "r": []})
            ent["w"] = my
            ent["r"] = []
        return my

    def barrier(self):
        snap = dict(self.cnt)
        for e in self.ENGINES:
            waits = {s: v for s, v in snap.items() if v > 0}
            self.ops[e].append((waits, None, e, 1))
            self.cnt[e] += 1
        snap2 = {e: self.cnt[e] for e in self.ENGINES}
        for e in self.ENGINES:
            waits = {s: snap2[s] for s in self.ENGINES if s != e}
            self.ops[e].append((waits, None, e, 1))
            self.cnt[e] += 1
        self.state = {}
        self.regions = {}

    def final_wait(self, eng):
        waits = {s: v for s, v in self.cnt.items()
                 if s not in self.ENGINES and v > 0}
        self.ops[eng].append((waits, None, eng, 0))

    def emit(self, blk, sems):
        def runner(name):
            hwm = {}

            def body(eng):
                for waits, fn, sem, inc in self.ops[name]:
                    for s in sorted(waits):
                        v = waits[s]
                        if hwm.get(s, -1) >= v:
                            continue
                        eng.wait_ge(sems[s], v)
                        hwm[s] = v
                    if fn is None:
                        if inc:
                            eng.nop(nofuse=True).then_inc(sems[sem], inc)
                        continue
                    fn(eng).then_inc(sems[sem], inc)
            return body

        blk.sync(runner(SY))
        blk.tensor(runner(TE))
        blk.vector(runner(VE))
        blk.scalar(runner(SC))
        blk.gpsimd(runner(GP))


def bankA(c0, c1):
    return [("psA", b) for b in range(c0 // 512, (c1 + 511) // 512)]


def bankB(c0, c1):
    return [("psB", b) for b in range(c0 // 512, (c1 + 511) // 512)]


def build(S=2048):
    assert S % 512 == 0
    NCH1 = S // 512
    S2 = min(1024, S)
    NCH2 = S // S2
    NVT = S // 128
    NTQ = S // 512

    nc = bass.Bass("TRN2", target_bir_lowering=False)

    din = lambda n, sh: nc.dram_tensor(n, sh, F32, kind="ExternalInput").ap()
    dtmp = lambda n, sh: nc.dram_tensor(n, sh, F32, kind="Internal").ap()

    x_fm = din("x_fm", [D, S])
    zq_t = din("zq_t", [D, D])
    zk_t = din("zk_t", [D, D])
    wv_T = din("wv_T", [D, D])
    wg_T = din("wg_T", [D, II])
    wu_T = din("wu_T", [D, II])
    wd_T = din("wd_T", [II, D])
    bqk_pp = din("bqk_pp", [128, 16])
    bv_row = din("bv_row", [1, D])
    sel64 = din("sel64", [16, D])
    eye = din("eye", [128, 128])
    consts = din("consts", [128, 8])
    out_fm = nc.dram_tensor("out_fm", [D, S], F32, kind="ExternalOutput").ap()

    q_sp = dtmp("q_sp", [D, S])
    k_sp = dtmp("k_sp", [D, S])
    xq_sp = dtmp("xq_sp", [D, S])
    xn_sp = dtmp("xn_sp", [D, S])
    at_sp = dtmp("at_sp", [D, S])
    wqg_d = dtmp("wqg_d", [D, II])
    wqu_d = dtmp("wqu_d", [D, II])
    wqd_d = dtmp("wqd_d", [II, D])
    hq_sp = dtmp("hq_sp", [II, S])
    rs_dr = dtmp("rs_dr", [16, S])

    P = Prog()

    ap_of = {}

    def sbuf(name, shape):
        h = nc.alloc_sbuf_tensor(name, shape, F32)
        ap_of[name] = h.ap()
        return ap_of[name]

    b_x = sbuf("b_x", [128, 7040])
    b_xn = sbuf("b_xn", [128, 4096])
    b_qk = sbuf("b_qk", [128, 8192])
    b_vt = sbuf("b_vt", [128, max(NVT * VTW, 8 * S, IT * 512)])
    b_pw = sbuf("b_pw", [128, 2048])
    b_scr = sbuf("b_scr", [128, 3072])
    b_at = sbuf("b_at", [128, 8192])
    b_sm = sbuf("b_sm", [128, 2080])
    b_ct = sbuf("b_ct", [128, 1408])

    rem = nc.sbuf_bytes_remaining

    CONSTVALS = [1e-5, -0.5, 0.5, 2.0, -2.0, 4.0, 0.125, -1.0]
    for i, v in enumerate(CONSTVALS):
        nc.const_aps.register(F32, float(v), b_sm[:, 1024 + i:1024 + i + 1])

    ones_col = b_sm[:, 0:1]
    ones_row = b_sm[0:1, 2:130]
    c_bqk = b_sm[:, 130:146]
    pc = b_sm[:, 146:210]
    colsq = b_sm[:, 210:218]
    zn_s = b_sm[:, 218:226]
    e2_s = b_sm[:, 226:234]
    em_s = b_sm[:, 234:242]
    rzn_s = b_sm[:, 242:250]
    bv_rep = b_sm[:, 1040:2064]

    t_eye = sbuf("t_eye", [128, 128])

    psA = nc.alloc_psum_tensor("psA", [128, 2048], F32).ap()
    psB = nc.alloc_psum_tensor("psB", [128, 2048], F32).ap()

    P.op(VE, lambda e: e.memset(ones_col, 1.0), writes=["ones"])
    P.op(VE, lambda e: e.memset(ones_row, 1.0), writes=["ones"])
    P.op(SY, lambda e: e.dma_start(out=b_sm[:, 1024:1032], in_=consts[:]),
         writes=["consts"], dma="cc")
    P.op(SY, lambda e: e.dma_start(out=c_bqk, in_=bqk_pp[:]), writes=["bqk"], dma="cc")
    P.op(SY, lambda e: e.dma_start(out=t_eye[:], in_=eye[:]), writes=["eye"], dma="cc")
    bvr = b_xn[0:1, 2304:3328]
    P.op(SY, lambda e: e.dma_start(out=bvr, in_=bv_row[:]), writes=["bvr"], dma="cc")
    for c in range(2):
        P.op(TE, lambda e, c=c: e.matmul(psA[:, c * 512:(c + 1) * 512],
                                         ones_row[0:1, 0:128],
                                         bvr[0:1, c * 512:(c + 1) * 512],
                                         start=True, stop=True),
             reads=["ones", "bvr"], writes=[("psA", c)])
        P.op(VE, lambda e, c=c: e.tensor_copy(bv_rep[:, c * 512:(c + 1) * 512],
                                              psA[:, c * 512:(c + 1) * 512]),
             reads=[("psA", c)], writes=["bvrep"])

    def poin_consts(zt, bco, pco):
        for mt in range(MT):
            w = b_pw[:, (mt % 2) * 1024:(mt % 2) * 1024 + 1024]
            P.op(SY, lambda e, mt=mt, w=w, zt=zt: e.dma_start(
                out=w, in_=zt[mt * 128:(mt + 1) * 128, :]),
                writes=[("pw", mt % 2)], dma=f"zw{mt % 2}")
            for kt in range(KT):
                P.op(SC, lambda e, w=w, kt=kt: e.activation(
                    b_scr[:, 0:128], w[:, kt * 128:(kt + 1) * 128], AF.Square),
                    reads=[("pw", mt % 2)], writes=["sc0"])
                P.op(TE, lambda e, kt=kt: e.matmul(
                    psB[0:1, 0:128], ones_col, b_scr[:, 0:128],
                    start=(kt == 0), stop=(kt == KT - 1)),
                    reads=["sc0", "ones"], writes=[("psB", 0)])
            P.op(VE, lambda e: e.tensor_copy(b_scr[0:1, 512:640], psB[0:1, 0:128]),
                 reads=[("psB", 0)], writes=["sc1"])
            P.op(TE, lambda e, mt=mt: e.transpose(
                psA[:, 512 + mt:513 + mt], b_scr[0:1, 512:640],
                t_eye[0:1, 0:1]),
                reads=["sc1", "eye"], writes=[("psA", 1)])
        P.op(VE, lambda e: e.tensor_copy(colsq, psA[:, 512:520]),
             reads=[("psA", 1)], writes=["colsq"])
        P.op(SC, lambda e: e.activation(zn_s, colsq, AF.Ln),
             reads=["colsq"], writes=["zn"])
        P.op(SC, lambda e: e.activation(zn_s, zn_s, AF.Exp, scale=0.5),
             reads=["zn", "consts"], writes=["zn"])
        P.op(VE, lambda e: e.tensor_scalar(zn_s, zn_s, 1e-15, None, A.max),
             reads=["zn"], writes=["zn"])
        bc = c_bqk[:, bco:bco + 8]
        P.op(SC, lambda e, bc=bc: e.activation(e2_s, bc, AF.Exp, scale=2.0),
             reads=["bqk", "consts"], writes=["e2"])
        P.op(SC, lambda e, bc=bc: e.activation(em_s, bc, AF.Exp, scale=-2.0),
             reads=["bqk", "consts"], writes=["em"])
        c1 = pc[:, pco:pco + 8]
        s2r = pc[:, pco + 8:pco + 16]
        t2 = pc[:, pco + 16:pco + 24]
        nt2 = pc[:, pco + 24:pco + 32]
        P.op(VE, lambda e: e.reciprocal(rzn_s, zn_s), reads=["zn"], writes=["rzn"])
        P.op(VE, lambda e, s2r=s2r: e.tensor_sub(s2r, e2_s, em_s),
             reads=["e2", "em"], writes=["pcs"])
        P.op(VE, lambda e, s2r=s2r: e.tensor_scalar(s2r, s2r, 0.5, None, A.mult),
             reads=["pcs"], writes=["pcs"])
        P.op(VE, lambda e, c1=c1: e.tensor_add(c1, e2_s, em_s),
             reads=["e2", "em"], writes=["pcc"])
        P.op(VE, lambda e, c1=c1: e.tensor_scalar(c1, c1, 0.5, None, A.mult),
             reads=["pcc"], writes=["pcc"])
        P.op(VE, lambda e, c1=c1: e.tensor_mul(c1, c1, rzn_s),
             reads=["pcc", "rzn"], writes=["pcc"])
        P.op(VE, lambda e, t2=t2: e.tensor_scalar(t2, zn_s, 2.0, None, A.mult),
             reads=["zn"], writes=["pct"])
        P.op(VE, lambda e, nt2=nt2: e.tensor_scalar(nt2, zn_s, -2.0, None, A.mult),
             reads=["zn"], writes=["pcn"])

    poin_consts(zq_t, 0, 0)
    poin_consts(zk_t, 8, 32)

    P.barrier()

    nxin = b_scr[:, 0:512]
    nxt = b_scr[:, 512:1024]
    nxu = b_scr[:, 1024:1536]
    nxa = b_scr[:, 1536:2048]
    nxg = b_scr[:, 2048:2560]
    nxh = b_scr[:, 2560:3072]
    nxbm = b_ct[:, 0:32]
    nxsc = b_ct[:, 32:64]

    def emit_qtile(src, dst, r0, cs, W, SL=None, NM="nx_", ch="xn0", qeng=GP,
                   preloaded=False, nxt_u=None):
        nb = W // 16
        if SL is None:
            _in, _t, _u, _a, _g, _h = nxin, nxt, nxu, nxa, nxg, nxh
            _bm, _sc = nxbm, nxsc
        else:
            _in, _t, _u, _a, _g, _h, _bm, _sc = SL
        K_in, K_t, K_u, K_a, K_g, K_h, K_bm, K_sc = (
            NM + "in", NM + "t", NM + "u", NM + "a", NM + "g", NM + "h",
            NM + "bm", NM + "sc")
        def q_inload(src, dst, r0, cs, W):
            P.op(qeng, lambda e: e.dma_start(
                out=_in[:, 0:W], in_=src[r0:r0 + 128, cs:cs + W]),
                writes=[K_in], dma=ch)
        if not preloaded:
            q_inload(src, dst, r0, cs, W)
        P.op(VE, lambda e: e.transpose(_t[:, 0:W], _in[:, 0:W]),
             reads=[K_in], writes=[K_t])
        if nxt_u is not None:
            q_inload(*nxt_u)
        P.op(VE, lambda e: e.reduce_max(
            _bm[:, 0:nb], _t[:, 0:W].rearrange("p (b c) -> p b c", c=16),
            AX.X, apply_absolute_value=True),
            reads=[K_t], writes=[K_bm])
        P.op(VE, lambda e: e.tensor_scalar(_sc[:, 0:nb], _bm[:, 0:nb],
                                           1.0 / 6.0, 1e-12, A.mult, A.max),
             reads=[K_bm], writes=[K_sc])
        P.op(VE, lambda e: e.reciprocal(_bm[:, 0:nb], _sc[:, 0:nb]),
             reads=[K_sc], writes=[K_bm])
        scb = _sc[:, 0:nb].rearrange("p (b o) -> p b o", o=1) \
            .broadcast_to([128, nb, 16])
        rcb = _bm[:, 0:nb].rearrange("p (b o) -> p b o", o=1) \
            .broadcast_to([128, nb, 16])
        P.op(VE, lambda e: e.tensor_tensor(
            _u[:, 0:W].rearrange("p (b c) -> p b c", c=16),
            _t[:, 0:W].rearrange("p (b c) -> p b c", c=16), rcb, A.mult),
            reads=[K_t, K_bm], writes=[K_u])
        P.op(VE, lambda e: e.scalar_tensor_tensor(
            _a[:, 0:W], _u[:, 0:W], -1.0, _u[:, 0:W], A.mult, A.max),
            reads=[K_u], writes=[K_a])
        P.op(VE, lambda e: e.tensor_scalar(_g[:, 0:W], _a[:, 0:W],
                                           0.25, None, A.is_gt),
             reads=[K_a], writes=[K_g])
        for th in (0.75, 1.25, 1.75):
            P.op(VE, lambda e, th=th: e.scalar_tensor_tensor(
                _g[:, 0:W], _a[:, 0:W], th, _g[:, 0:W], A.is_gt, A.add),
                reads=[K_a, K_g], writes=[K_g])
        P.op(VE, lambda e: e.tensor_scalar(_h[:, 0:W], _a[:, 0:W],
                                           2.5, None, A.is_gt),
             reads=[K_a], writes=[K_h])
        P.op(VE, lambda e: e.scalar_tensor_tensor(
            _h[:, 0:W], _a[:, 0:W], 3.5, _h[:, 0:W], A.is_gt, A.add),
            reads=[K_a, K_h], writes=[K_h])
        P.op(VE, lambda e: e.scalar_tensor_tensor(
            _g[:, 0:W], _g[:, 0:W], 0.5, _h[:, 0:W], A.mult, A.add),
            reads=[K_g, K_h], writes=[K_g])
        P.op(VE, lambda e: e.tensor_scalar(_h[:, 0:W], _a[:, 0:W],
                                           5.0, None, A.is_gt),
             reads=[K_a], writes=[K_h])
        P.op(VE, lambda e: e.scalar_tensor_tensor(
            _g[:, 0:W], _h[:, 0:W], 2.0, _g[:, 0:W], A.mult, A.add),
            reads=[K_h, K_g], writes=[K_g])
        P.op(VE, lambda e: e.tensor_tensor(
            _t[:, 0:W].rearrange("p (b c) -> p b c", c=16),
            _g[:, 0:W].rearrange("p (b c) -> p b c", c=16), scb, A.mult),
            reads=[K_g, K_sc], writes=[K_t])
        P.op(VE, lambda e: e.tensor_scalar(_a[:, 0:W], _u[:, 0:W],
                                           0.0, 2.0, A.is_ge, A.mult),
             reads=[K_u], writes=[K_a])
        P.op(VE, lambda e: e.scalar_tensor_tensor(
            _g[:, 0:W], _a[:, 0:W], 1.0, _t[:, 0:W], A.subtract, A.mult),
            reads=[K_a, K_t], writes=[K_g])
        P.op(VE, lambda e: e.transpose(_u[:, 0:W], _g[:, 0:W]),
             reads=[K_g], writes=[K_u])
        P.op(GP, lambda e: e.dma_start(
            out=dst[r0:r0 + 128, cs:cs + W], in_=_u[:, 0:W]),
            reads=[K_u], dma="sp")


    _qw = []
    for _wsrc, _wdst, _R, _C in [(wg_T, wqg_d, D, II), (wu_T, wqu_d, D, II),
                                 (wd_T, wqd_d, II, D)]:
        for _rt in range(_R // 128):
            _c0 = 0
            while _c0 < _C:
                _W = min(512, _C - _c0)
                _qw.append((_wsrc, _wdst, _rt * 128, _c0, _W))
                _c0 += _W
    N_P1Q = min(6 * NCH1, len(_qw))
    _qw_p1 = _qw[:N_P1Q]
    _qw_rest = _qw[N_P1Q:-22]
    _qw_c1 = _qw[-22:]
    p1q_slots = (b_at[:, 5120:5632], b_at[:, 5632:6144], b_at[:, 6144:6656],
                 b_at[:, 6656:7168], b_at[:, 7168:7680], b_at[:, 7680:8192],
                 b_ct[:, 64:96], b_ct[:, 96:128])
    p1q_next = [0]

    def p1_qunit(n=1):
        for _ in range(n):
            i = p1q_next[0]
            if i < len(_qw_p1):
                emit_qtile(*_qw_p1[i], SL=p1q_slots, NM="pq_",
                           ch="pq0", qeng=SY, preloaded=(i > 0),
                           nxt_u=_qw_p1[i + 1]
                           if i + 1 < len(_qw_p1) else None)
                p1q_next[0] += 1

    x_t = [b_x[:, kt * 512:(kt + 1) * 512] for kt in range(KT)]
    xn_t = [b_xn[:, kt * 512:(kt + 1) * 512] for kt in range(KT)]
    q_t = [b_qk[:, mt * 512:(mt + 1) * 512] for mt in range(MT)]
    k_t = [b_qk[:, 4096 + mt * 512:4096 + (mt + 1) * 512] for mt in range(MT)]
    vt_t = [b_vt[:, i * VTW:(i + 1) * VTW] for i in range(NVT)]
    r_m = b_at[0:1, 0:512]
    r_t = b_at[0:1, 512:1024]
    r_var = b_at[0:1, 1024:1536]
    r_rstd = b_at[0:1, 1536:2048]
    r_mr = b_at[0:1, 2048:2560]
    r_lam = b_at[0:1, 2560:3072]
    r_lam1 = b_at[0:1, 3072:3584]
    r_d = b_at[0:1, 3584:4096]
    lam_rep = b_at[:, 4096:4608]
    lam1_rep = b_at[:, 4608:5120]
    s_a = b_scr[:, 0:512]
    s_b = b_scr[:, 512:1024]
    s_c = b_scr[:, 1024:1536]
    s_d2 = b_scr[:, 1536:2048]
    s_e = b_scr[:, 2048:2560]

    ssets = [
        dict(a=b_scr[:, 0:512], b=b_scr[:, 512:1024], c=b_scr[:, 1024:1536],
             d=b_scr[:, 1536:2048], e=b_scr[:, 2048:2560]),
        dict(a=b_scr[:, 2560:3072], b=b_x[:, 4096:4608],
             c=b_x[:, 4608:5120], d=b_x[:, 5120:5632],
             e=b_x[:, 5632:6144]),
    ]

    def proj_poincare(proj, zt, pco, spill, tcs, tail_hook=None):
        qt = q_t if proj == "q" else k_t

        def emit_mms(mt):
            wb = b_pw[:, (mt % 2) * 1024:(mt % 2) * 1024 + 1024]
            P.op(SY, lambda e, mt=mt, wb=wb, zt=zt: e.dma_start(
                out=wb, in_=zt[mt * 128:(mt + 1) * 128, :]),
                writes=[("pw", mt % 2)], dma=f"zw{mt % 2}")
            pv = psB[:, 1024 + (mt % 2) * 512:1536 + (mt % 2) * 512]
            pvk = ("psB", 2 + mt % 2)
            for kt in range(KT):
                P.op(TE, lambda e, wb=wb, kt=kt, pv=pv: e.matmul(
                    pv, wb[:, kt * 128:(kt + 1) * 128], xn_t[kt],
                    start=(kt == 0), stop=(kt == KT - 1)),
                    reads=[("pw", mt % 2), ("xnt", kt)], writes=[pvk])
            return pv, pvk

        def chain_stages(mt, si, pv, pvk):
            ss = ssets[si]
            sfx = str(si)
            c1 = pc[:, pco + mt:pco + mt + 1]
            s2r = pc[:, pco + 8 + mt:pco + 8 + mt + 1]
            t2 = pc[:, pco + 16 + mt:pco + 16 + mt + 1]
            nt2 = pc[:, pco + 24 + mt:pco + 24 + mt + 1]
            sa, sb2, sc3, sd, se = ss["a"], ss["b"], ss["c"], ss["d"], ss["e"]
            yield lambda: P.op(VE, lambda e: e.scalar_tensor_tensor(
                sa, pv, c1, lam_rep, A.mult, A.mult),
                reads=[pvk, "lamr", "pcc"], writes=["sa" + sfx])
            yield lambda: P.op(VE, lambda e: e.scalar_tensor_tensor(
                sb2, lam1_rep, s2r, sa, A.mult, A.subtract),
                reads=["lam1r", "sa" + sfx, "pcs"], writes=["sb" + sfx])
            yield lambda: P.op(SC, lambda e: e.activation(sc3, sb2, AF.Square),
                               reads=["sb" + sfx], writes=["sc" + sfx])
            yield lambda: P.op(SC, lambda e: e.activation(sc3, sc3, AF.Ln,
                                                          bias=1.0),
                               reads=["sc" + sfx], writes=["sc" + sfx])
            yield lambda: P.op(SC, lambda e: e.activation(sc3, sc3, AF.Exp,
                                                          scale=0.5),
                               reads=["sc" + sfx, "consts"], writes=["sc" + sfx])
            yield lambda: P.op(VE, lambda e: e.tensor_add(sc3, sb2, sc3),
                               reads=["sb" + sfx, "sc" + sfx], writes=["sc" + sfx])
            yield lambda: P.op(SC, lambda e: e.activation(sc3, sc3, AF.Ln),
                               reads=["sc" + sfx], writes=["sc" + sfx])
            yield lambda: P.op(SC, lambda e: e.activation(sd, sc3, AF.Exp,
                                                          scale=t2),
                               reads=["sc" + sfx, "pct"], writes=["sd" + sfx])
            yield lambda: P.op(SC, lambda e: e.activation(se, sc3, AF.Exp,
                                                          scale=nt2),
                               reads=["sc" + sfx, "pcn"], writes=["se" + sfx])
            yield lambda: P.op(VE, lambda e: e.tensor_sub(qt[mt], sd, se),
                               reads=["sd" + sfx, "se" + sfx],
                               writes=[("qt", (proj, mt))])
            yield lambda: P.op(SC, lambda e: e.activation(sa, qt[mt],
                                                          AF.Square),
                               reads=[("qt", (proj, mt))], writes=["sa" + sfx])
            yield lambda: P.op(TE, lambda e, mt=mt: e.matmul(
                psB[0:1, 0:512], ones_col, sa,
                start=(mt == 0), stop=(mt == MT - 1)),
                reads=["sa" + sfx, "ones"], writes=[("psB", 0)])

        pending_tail = []
        for b0 in range(0, MT, 2):
            pv0, pvk0 = emit_mms(b0)
            pv1, pvk1 = emit_mms(b0 + 1)
            if b0 == 0 and tail_hook is not None:
                tail_hook()
            for t0, t1 in pending_tail:
                t0()
                t1()
            pending_tail = []
            g0 = list(chain_stages(b0, 0, pv0, pvk0))
            g1 = list(chain_stages(b0 + 1, 1, pv1, pvk1))
            for st0, st1 in zip(g0[:-2], g1[:-2]):
                st0()
                st1()
            pending_tail = [(g0[-2], g1[-2]), (g0[-1], g1[-1])]
        for t0, t1 in pending_tail:
            t0()
            t1()
        rd_ps = psA[:, 0:512] if proj == "q" else psB[:, 0:512]
        rd_key = ("psA", 0) if proj == "q" else ("psB", 0)

        def tail():
            P.op(SC, lambda e: e.activation(r_d, psB[0:1, 0:512], AF.Ln, bias=4.0),
                 reads=[("psB", 0), "consts"], writes=["r_d"])
            P.op(SC, lambda e: e.activation(r_d, r_d, AF.Exp, scale=0.5),
                 reads=["r_d", "consts"], writes=["r_d"])
            P.op(VE, lambda e: e.tensor_scalar(r_d, r_d, 2.0, None, A.add),
                 reads=["r_d"], writes=["r_d"])
            P.op(VE, lambda e: e.reciprocal(r_d, r_d), reads=["r_d"], writes=["r_d"])
            P.op(TE, lambda e: e.matmul(rd_ps, ones_row, r_d,
                                        start=True, stop=True),
                 reads=["r_d", "ones"], writes=[rd_key])
            for mt in range(MT):
                P.op(VE, lambda e, qt=qt, mt=mt: e.tensor_tensor(
                    qt[mt], qt[mt], rd_ps, A.mult),
                    reads=[("qt", (proj, mt)), rd_key],
                    writes=[("qt", (proj, mt))])
                P.op(GP, lambda e, qt=qt, mt=mt, spill=spill, tcs=tcs: e.dma_start(
                    out=spill[mt * 128:(mt + 1) * 128, tcs], in_=qt[mt]),
                    reads=[("qt", (proj, mt))], dma="sp")
        return tail


    scr_sq = b_x[:, 6144:6656]

    def p1_xload(tc):
        tcs = slice(tc * 512, (tc + 1) * 512)
        for kt in range(KT):
            P.op(SY, lambda e, kt=kt, tcs=tcs: e.dma_start(
                out=x_t[kt], in_=x_fm[kt * 128:(kt + 1) * 128, tcs]),
                writes=[("xt", kt)], dma="xl")

    def p1_stats():
        for kt in range(KT):
            P.op(SC, lambda e, kt=kt: e.activation(scr_sq, x_t[kt], AF.Square),
                 reads=[("xt", kt)], writes=["sqx"])
            P.op(TE, lambda e, kt=kt: e.matmul(psB[0:1, 0:512], ones_col, scr_sq,
                                               start=(kt == 0), stop=(kt == KT - 1)),
                 reads=["sqx", "ones"], writes=[("psB", 0)])
            P.op(TE, lambda e, kt=kt: e.matmul(psB[0:1, 512:1024], ones_col,
                                               x_t[kt],
                                               start=(kt == 0), stop=(kt == KT - 1)),
                 reads=[("xt", kt), "ones"], writes=[("psB", 1)])

    def p1_rows():
        P.op(VE, lambda e: e.tensor_scalar(r_m, psB[0:1, 512:1024], 1.0 / D,
                                           None, A.mult),
             reads=[("psB", 1)], writes=["r_m"])
        P.op(VE, lambda e: e.tensor_mul(r_t, r_m, r_m), reads=["r_m"], writes=["r_t"])
        P.op(VE, lambda e: e.scalar_tensor_tensor(r_var, psB[0:1, 0:512],
                                                  1.0 / D, r_t, A.mult, A.subtract),
             reads=[("psB", 0), "r_t"], writes=["r_var"])
        P.op(SC, lambda e: e.activation(r_rstd, r_var, AF.Ln, bias=1e-5),
             reads=["r_var", "consts"], writes=["r_rstd"])
        P.op(SC, lambda e: e.activation(r_rstd, r_rstd, AF.Exp, scale=-0.5),
             reads=["r_rstd", "consts"], writes=["r_rstd"])
        P.op(VE, lambda e: e.tensor_mul(r_mr, r_m, r_rstd),
             reads=["r_m", "r_rstd"], writes=["r_mr"])
        P.op(VE, lambda e: e.tensor_scalar(r_t, r_var, 1e-5, None, A.add),
             reads=["r_var"], writes=["r_t"])
        P.op(VE, lambda e: e.reciprocal(r_t, r_t), reads=["r_t"], writes=["r_t"])
        P.op(VE, lambda e: e.tensor_mul(r_t, r_var, r_t),
             reads=["r_var", "r_t"], writes=["r_t"])
        P.op(VE, lambda e: e.tensor_scalar(r_t, r_t, -float(D), 1.0, A.mult, A.add),
             reads=["r_t"], writes=["r_t"])
        P.op(VE, lambda e: e.reciprocal(r_t, r_t), reads=["r_t"], writes=["r_t"])
        P.op(VE, lambda e: e.tensor_scalar(r_lam, r_t, 2.0, None, A.mult),
             reads=["r_t"], writes=["r_lam"])
        P.op(VE, lambda e: e.tensor_scalar(r_lam1, r_lam, -1.0, None, A.add),
             reads=["r_lam"], writes=["r_lam1"])

    def p1_reps_xn():
        P.op(TE, lambda e: e.matmul(psA[:, 0:512], ones_row, r_rstd,
                                    start=True, stop=True),
             reads=["r_rstd", "ones"], writes=[("psA", 0)])
        P.op(TE, lambda e: e.matmul(psA[:, 512:1024], ones_row, r_mr,
                                    start=True, stop=True),
             reads=["r_mr", "ones"], writes=[("psA", 1)])
        P.op(TE, lambda e: e.matmul(psA[:, 1024:1536], ones_row, r_lam,
                                    start=True, stop=True),
             reads=["r_lam", "ones"], writes=[("psA", 2)])
        P.op(TE, lambda e: e.matmul(psA[:, 1536:2048], ones_row, r_lam1,
                                    start=True, stop=True),
             reads=["r_lam1", "ones"], writes=[("psA", 3)])
        P.op(VE, lambda e: e.tensor_copy(lam_rep, psA[:, 1024:1536]),
             reads=[("psA", 2)], writes=["lamr"])
        P.op(VE, lambda e: e.tensor_copy(lam1_rep, psA[:, 1536:2048]),
             reads=[("psA", 3)], writes=["lam1r"])
        for kt in range(KT):
            P.op(VE, lambda e, kt=kt: e.tensor_tensor(xn_t[kt], x_t[kt],
                                                      psA[:, 0:512], A.mult),
                 reads=[("xt", kt), ("psA", 0)], writes=[("xnt", kt)])
            P.op(VE, lambda e, kt=kt: e.tensor_tensor(xn_t[kt], xn_t[kt],
                                                      psA[:, 512:1024], A.subtract),
                 reads=[("xnt", kt), ("psA", 1)], writes=[("xnt", kt)])

    def p1_v_wave(tc, wave, hook=None):
        for kt in range(KT):
            if kt == 2 and hook is not None:
                hook()
            wb = b_pw[:, (kt % 2) * 1024:(kt % 2) * 1024 + 1024]
            P.op(SY, lambda e, kt=kt, wb=wb: e.dma_start(
                out=wb, in_=wv_T[kt * 128:(kt + 1) * 128, :]),
                writes=[("pw", kt % 2)], dma=f"zw{kt % 2}")
            for tt in (wave * 2, wave * 2 + 1):
                for nh2 in range(2):
                    bank = (tt % 2) * 2 + nh2
                    pvv = psA[:, bank * 512:(bank + 1) * 512]
                    P.op(TE, lambda e, wb=wb, kt=kt, tt=tt, nh2=nh2, pvv=pvv:
                         e.matmul(
                             pvv, xn_t[kt][:, tt * 128:(tt + 1) * 128],
                             wb[:, nh2 * 512:(nh2 + 1) * 512],
                             start=(kt == 0), stop=(kt == KT - 1)),
                         reads=[("pw", kt % 2), ("xnt", kt)],
                         writes=[("psA", bank)])
        for tt in (wave * 2, wave * 2 + 1):
            vi = tc * 4 + tt
            P.op(VE, lambda e, vi=vi: e.memset(
                vt_t[vi].rearrange("p (h w) -> p h w", w=65)[:, :, 64:65], 1.0),
                writes=[("vt", vi)])
            for nh2 in range(2):
                bank = (tt % 2) * 2 + nh2
                pvv = psA[:, bank * 512:(bank + 1) * 512]
                P.op(VE, lambda e, vi=vi, nh2=nh2, pvv=pvv: e.scalar_tensor_tensor(
                    vt_t[vi].rearrange("p (h w) -> p h w", w=65)
                    [:, nh2 * 8:(nh2 + 1) * 8, 0:64],
                    pvv.rearrange("p (h d) -> p h d", d=64), 1.0,
                    bv_rep[:, nh2 * 512:(nh2 + 1) * 512]
                    .rearrange("p (h d) -> p h d", d=64),
                    A.mult, A.add),
                    reads=[("psA", bank), "bvrep"], writes=[("vt", vi)])

    p1_xload(0)
    p1_stats()
    p1_rows()
    for tc in range(NCH1):
        tcs = slice(tc * 512, (tc + 1) * 512)
        p1_reps_xn()
        tq_tail = proj_poincare("q", zq_t, 0, q_sp, tcs)
        p1_qunit(2)
        tk_tail = proj_poincare("k", zk_t, 32, k_sp, tcs, tail_hook=tq_tail)
        if tc + 1 < NCH1:
            p1_xload(tc + 1)
        p1_qunit(2)
        p1_v_wave(tc, 0, hook=tk_tail)
        if tc + 1 < NCH1:
            p1_stats()
        p1_qunit(1)
        p1_v_wave(tc, 1)
        p1_qunit(1)
        if tc + 1 < NCH1:
            p1_rows()
        for kt in range(KT):
            P.op(GP, lambda e, kt=kt, tcs=tcs: e.dma_start(
                out=xn_sp[kt * 128:(kt + 1) * 128, tcs], in_=xn_t[kt]),
                reads=[("xnt", kt)], dma="sp")

    P.barrier()

    SC2 = 512
    NC2 = S // SC2
    K2buf = [b_qk[:, 0:S], b_qk[:, S:2 * S]]
    Q2buf = [b_qk[:, 2 * S:2 * S + 512], b_qk[:, 2 * S + 512:2 * S + 1024]]
    eslots = []
    for buf, c0, c1 in [(b_x, 0, 7040), (b_qk, 2 * S + 1024, 8192),
                        (b_pw, 0, 2048), (b_at, 4096, 8192),
                        (b_xn, 2560, 4096)]:
        c = c0
        while c + 512 <= c1:
            eslots.append(buf[:, c:c + 512])
            c += 512
    assert len(eslots) >= 2 * NVT, (len(eslots), NVT)
    Et = [[eslots[p * NVT + tk] for tk in range(NVT)] for p in range(2)]
    at_t = [b_at[:, mt * 512:(mt + 1) * 512] for mt in range(MT)]
    rs_t = b_xn[0:16, 0:512]
    rs_scr = [b_xn[64:65, 1536:2048], b_xn[64:65, 2048:2560]]

    qunits = list(_qw_rest)
    for tc in range(NCH1):
        for kt in range(KT):
            qunits.append((xn_sp, xq_sp, kt * 128, tc * 512, 512))

    per_iter = (len(qunits) + NC2 * 8 - 1) // (NC2 * 8)
    q_next = [0]

    def emit_unit():
        i = q_next[0]
        if i < len(qunits):
            emit_qtile(*qunits[i], preloaded=(i > 0),
                       nxt_u=qunits[i + 1] if i + 1 < len(qunits) else None)
            q_next[0] += 1
            return True
        return False

    KP = max(1, NVT // 4)

    def load_pair(ch, hp):
        chs2 = slice(ch * SC2, (ch + 1) * SC2)
        P.op(SY, lambda e, hp=hp, chs2=chs2: e.dma_start(
            out=Q2buf[hp % 2], in_=q_sp[hp * 128:(hp + 1) * 128, chs2]),
            writes=[("qb", hp % 2)], dma=f"q{hp % 2}")
        for pi in range(NVT // KP):
            csl = slice(pi * KP * 128, (pi + 1) * KP * 128)
            P.op(SY, lambda e, hp=hp, csl=csl: e.dma_start(
                out=K2buf[hp % 2][:, csl], in_=k_sp[hp * 128:(hp + 1) * 128, csl]),
                writes=[("kb", (hp % 2, pi))], dma=f"k{hp % 2}p{pi}")

    load_pair(0, 0)
    for ch in range(NC2):
        chs = slice(ch * SC2, (ch + 1) * SC2)
        for hp in range(8):
            if hp < 7:
                load_pair(ch, hp + 1)
            elif ch + 1 < NC2:
                load_pair(ch + 1, 0)
            for _ in range(per_iter):
                emit_unit()
            for tk in range(NVT):
                for par in range(2):
                    bank = par * 2 + (tk % 2)
                    P.op(TE, lambda e, hp=hp, tk=tk, par=par, bank=bank: e.matmul(
                        psA[:, bank * 512:(bank + 1) * 512],
                        K2buf[hp % 2][par * 64:(par + 1) * 64,
                                      tk * 128:(tk + 1) * 128],
                        Q2buf[hp % 2][par * 64:(par + 1) * 64, :],
                        start=True, stop=True),
                        reads=[("kb", (hp % 2, tk // KP)), ("qb", hp % 2)],
                        writes=[("psA", bank)])
                for par in range(2):
                    bank = par * 2 + (tk % 2)
                    P.op(SC, lambda e, tk=tk, par=par, bank=bank: e.activation(
                        Et[par][tk], psA[:, bank * 512:(bank + 1) * 512],
                        AF.Exp, scale=0.125),
                        reads=[("psA", bank), "consts"],
                        writes=[("eb", (par, tk))])
            for tk in range(NVT):
                for par in range(2):
                    P.op(TE, lambda e, hp=hp, tk=tk, par=par: e.matmul(
                        psB[0:65, par * 512:(par + 1) * 512],
                        vt_t[tk][:, (2 * hp + par) * 65:(2 * hp + par) * 65 + 65],
                        Et[par][tk],
                        start=(tk == 0), stop=(tk == NVT - 1)),
                        reads=[("eb", (par, tk)), ("vt", tk)],
                        writes=[("psB", par)])
            for par in range(2):
                P.op(SC, lambda e, hp=hp, par=par: e.activation(
                    at_t[hp][par * 64:(par + 1) * 64, :],
                    psB[0:64, par * 512:(par + 1) * 512], AF.Copy),
                    reads=[("psB", par)], writes=[("at", hp)])
                P.op(SC, lambda e, par=par: e.activation(
                    rs_scr[par], psB[64:65, par * 512:(par + 1) * 512],
                    AF.Copy),
                    reads=[("psB", par)], writes=[("rss", par)])
                P.op(SY, lambda e, hp=hp, par=par: e.dma_start(
                    out=rs_t[2 * hp + par:2 * hp + par + 1, :],
                    in_=rs_scr[par]),
                    reads=[("rss", par)], writes=[("rs", par)],
                    dma=f"rs{par}")
            P.op(SY, lambda e, hp=hp, chs=chs: e.dma_start(
                out=at_sp[hp * 128:(hp + 1) * 128, chs], in_=at_t[hp]),
                reads=[("at", hp)], dma="at")
        P.op(SY, lambda e, chs=chs: e.dma_start(
            out=rs_dr[:, chs], in_=rs_t), reads=["rs"], dma="rsd")
    while emit_unit():
        pass

    P.barrier()

    xq_t = [b_vt[:, kt * S:(kt + 1) * S] for kt in range(KT)]
    for kt in range(KT):
        P.op(SY, lambda e, kt=kt: e.dma_start(
            out=xq_t[kt], in_=xq_sp[kt * 128:(kt + 1) * 128, :]),
            writes=[("xqt", kt)], dma="xl")
    sg_t = b_xn[:, 0:S]
    h_t = b_xn[:, 2048:2048 + S] if S <= 2048 else None
    wgbuf = [b_qk[:, 0:1024], b_qk[:, 1024:2048]]
    wubuf = [b_qk[:, 2048:3072], b_qk[:, 3072:4096]]
    hq_t = b_qk[:, 4096:4096 + S]
    hbm = b_ct[:, 0:64]
    hsc2 = b_ct[:, 64:128]
    c1q_slots = (b_at[:, 2048:2560], b_at[:, 2560:3072],
                 b_at[:, 3072:3584], b_at[:, 3584:4096],
                 b_at[:, 4096:4608], b_at[:, 4608:5120],
                 b_ct[:, 128:160], b_ct[:, 160:192])
    c1q_next = [0]

    def c1_qunit(n=1):
        for _ in range(n):
            i = c1q_next[0]
            if i < len(_qw_c1):
                emit_qtile(*_qw_c1[i], SL=c1q_slots, NM="cq_", ch="cq0",
                           qeng=GP, preloaded=(i > 0),
                           nxt_u=_qw_c1[i + 1]
                           if i + 1 < len(_qw_c1) else None)
                c1q_next[0] += 1

    for m in range(IT):
        c1_qunit(1)
        P.op(SY, lambda e, m=m: e.dma_start(
            out=wgbuf[m % 2].rearrange("p (kt c) -> p kt c", c=128),
            in_=wqg_d.rearrange("(kt p) o -> p kt o", p=128)
            [:, :, m * 128:(m + 1) * 128]),
            writes=[("wgb", m % 2)], dma=f"wf{m % 2}")
        P.op(SY, lambda e, m=m: e.dma_start(
            out=wubuf[m % 2].rearrange("p (kt c) -> p kt c", c=128),
            in_=wqu_d.rearrange("(kt p) o -> p kt o", p=128)
            [:, :, m * 128:(m + 1) * 128]),
            writes=[("wub", m % 2)], dma=f"wf{m % 2}")
        for kt in range(KT):
            for n in range(NTQ):
                P.op(TE, lambda e, m=m, kt=kt, n=n: e.matmul(
                    psA[:, n * 512:(n + 1) * 512],
                    wgbuf[m % 2][:, kt * 128:(kt + 1) * 128],
                    xq_t[kt][:, n * 512:(n + 1) * 512],
                    start=(kt == 0), stop=(kt == KT - 1)),
                    reads=[("wgb", m % 2), ("xqt", kt)], writes=[("psA", n)])
                P.op(TE, lambda e, m=m, kt=kt, n=n: e.matmul(
                    psB[:, n * 512:(n + 1) * 512],
                    wubuf[m % 2][:, kt * 128:(kt + 1) * 128],
                    xq_t[kt][:, n * 512:(n + 1) * 512],
                    start=(kt == 0), stop=(kt == KT - 1)),
                    reads=[("wub", m % 2), ("xqt", kt)], writes=[("psB", n)])
        P.op(SC, lambda e: e.activation(sg_t, psA[:, 0:S], AF.Silu),
             reads=[("psA", n) for n in range(NTQ)], writes=["sg"])
        P.op(VE, lambda e: e.scalar_tensor_tensor(h_t, psB[:, 0:S], 1.0, sg_t,
                                                  A.mult, A.mult),
             reads=[("psB", n) for n in range(NTQ)] + ["sg"], writes=["h"])
        for hf in range(S // 1024 if S >= 1024 else 1):
            HW = min(1024, S)
            hsl = h_t[:, hf * HW:(hf + 1) * HW]
            t32 = b_scr[:, 0:HW]
            u = b_scr[:, 1024:1024 + HW]
            aa = b_scr[:, 2048:2048 + HW]
            g = b_at[:, 0:HW]
            h2 = b_at[:, 1024:1024 + HW]
            nb = HW // 16
            P.op(VE, lambda e, hsl=hsl, t32=t32: e.transpose(t32, hsl),
                 reads=["h"], writes=["hin"])
            P.op(VE, lambda e, t32=t32, nb=nb: e.reduce_max(
                hbm[:, 0:nb], t32.rearrange("p (b c) -> p b c", c=16), AX.X,
                apply_absolute_value=True),
                reads=["hin"], writes=["hbm"])
            P.op(VE, lambda e, nb=nb: e.tensor_scalar(
                hsc2[:, 0:nb], hbm[:, 0:nb], 1.0 / 6.0, 1e-12, A.mult, A.max),
                reads=["hbm"], writes=["hsc"])
            P.op(VE, lambda e, nb=nb: e.reciprocal(hbm[:, 0:nb], hsc2[:, 0:nb]),
                 reads=["hsc"], writes=["hbm"])
            scb = hsc2[:, 0:nb].rearrange("p (b o) -> p b o", o=1) \
                .broadcast_to([128, nb, 16])
            rcb = hbm[:, 0:nb].rearrange("p (b o) -> p b o", o=1) \
                .broadcast_to([128, nb, 16])
            P.op(VE, lambda e, u=u, t32=t32, rcb=rcb: e.tensor_tensor(
                u.rearrange("p (b c) -> p b c", c=16),
                t32.rearrange("p (b c) -> p b c", c=16), rcb, A.mult),
                reads=["hin", "hbm"], writes=["hu"])
            P.op(VE, lambda e, u=u, aa=aa: e.scalar_tensor_tensor(
                aa, u, -1.0, u, A.mult, A.max), reads=["hu"], writes=["ha"])
            P.op(VE, lambda e, aa=aa, g=g: e.tensor_scalar(
                g, aa, 0.25, None, A.is_gt), reads=["ha"], writes=["hg"])
            for th in (0.75, 1.25, 1.75):
                P.op(VE, lambda e, aa=aa, g=g, th=th: e.scalar_tensor_tensor(
                    g, aa, th, g, A.is_gt, A.add),
                    reads=["ha", "hg"], writes=["hg"])
            P.op(VE, lambda e, aa=aa, h2=h2: e.tensor_scalar(
                h2, aa, 2.5, None, A.is_gt), reads=["ha"], writes=["hh"])
            P.op(VE, lambda e, aa=aa, h2=h2: e.scalar_tensor_tensor(
                h2, aa, 3.5, h2, A.is_gt, A.add), reads=["ha", "hh"], writes=["hh"])
            P.op(VE, lambda e, g=g, h2=h2: e.scalar_tensor_tensor(
                g, g, 0.5, h2, A.mult, A.add), reads=["hg", "hh"], writes=["hg"])
            P.op(VE, lambda e, aa=aa, h2=h2: e.tensor_scalar(
                h2, aa, 5.0, None, A.is_gt), reads=["ha"], writes=["hh"])
            P.op(VE, lambda e, g=g, h2=h2: e.scalar_tensor_tensor(
                g, h2, 2.0, g, A.mult, A.add), reads=["hh", "hg"], writes=["hg"])
            P.op(VE, lambda e, t32=t32, g=g, scb=scb: e.tensor_tensor(
                t32.rearrange("p (b c) -> p b c", c=16),
                g.rearrange("p (b c) -> p b c", c=16), scb, A.mult),
                reads=["hg", "hsc"], writes=["hin"])
            P.op(SC, lambda e, u=u, aa=aa: e.sign(aa, u),
                 reads=["hu"], writes=["ha"])
            P.op(VE, lambda e, g=g, aa=aa, t32=t32: e.tensor_mul(g, t32, aa),
                 reads=["hin", "ha", "hg"], writes=["hg"])
            P.op(VE, lambda e, g=g, hf=hf, HW=HW: e.transpose(
                hq_t[:, hf * HW:(hf + 1) * HW], g), reads=["hg"], writes=["hqo"])
        P.op(GP, lambda e, m=m: e.dma_start(
            out=hq_sp[m * 128:(m + 1) * 128, :], in_=hq_t),
            reads=["hqo"], dma="sp")
    while c1q_next[0] < len(_qw_c1):
        c1_qunit(1)

    P.barrier()

    hqbuf = [b_vt[:, i * 512:(i + 1) * 512] for i in range(IT)]
    wdbuf = [b_qk[:, 0:1024], b_qk[:, 1024:2048]]
    att_s = [b_scr[:, 0:512], b_scr[:, 1536:2048]]
    x_s = [b_scr[:, 512:1024], b_scr[:, 2048:2560]]
    ob = b_scr[:, 1024:1536]
    def c2_hq(n, kt):
        ns2 = slice(n * 512, (n + 1) * 512)
        P.op(SY, lambda e, kt=kt, ns2=ns2: e.dma_start(
            out=hqbuf[kt], in_=hq_sp[kt * 128:(kt + 1) * 128, ns2]),
            writes=[("hqb", kt)], dma=f"hq{kt % 4}")

    def c2_wd(kt):
        P.op(SY, lambda e, kt=kt: e.dma_start(
            out=wdbuf[kt % 2], in_=wqd_d[kt * 128:(kt + 1) * 128, :]),
            writes=[("wdb", kt % 2)], dma=f"wf{kt % 2}")

    rs_rep = [b_at[:, 0:512], b_at[:, 512:1024]]
    obs = [ob, b_scr[:, 2560:3072]]

    def c2_rb(n, m):
        ns2 = slice(n * 512, (n + 1) * 512)
        for par in range(2):
            P.op(SY, lambda e, m=m, par=par, ns2=ns2: e.dma_start(
                out=rs_rep[m % 2][par * 64:(par + 1) * 64, :],
                in_=rs_dr[2 * m + par:2 * m + par + 1,
                          ns2].broadcast_to([64, 512])),
                writes=[("rsr", m % 2)], dma=f"rb{m % 2}")

    c2_hq(0, 0)
    c2_hq(0, 1)
    c2_wd(0)
    c2_rb(0, 0)
    for n in range(NTQ):
        ns = slice(n * 512, (n + 1) * 512)
        for kt in range(IT):
            if kt + 2 < IT:
                c2_hq(n, kt + 2)
            elif n + 1 < NTQ:
                c2_hq(n + 1, kt + 2 - IT)
            if kt + 1 < IT:
                c2_wd(kt + 1)
            elif n + 1 < NTQ:
                c2_wd(0)
            for m in range(MT):
                pp = psA if m < 4 else psB
                pk = ("psA" if m < 4 else "psB", m % 4)
                P.op(TE, lambda e, kt=kt, m=m, pp=pp: e.matmul(
                    pp[:, (m % 4) * 512:(m % 4) * 512 + 512],
                    wdbuf[kt % 2][:, m * 128:(m + 1) * 128], hqbuf[kt],
                    start=(kt == 0), stop=(kt == IT - 1)),
                    reads=[("wdb", kt % 2), ("hqb", kt)], writes=[pk])
        for m in range(MT):
            pp = psA if m < 4 else psB
            pk = ("psA" if m < 4 else "psB", m % 4)
            P.op(SY, lambda e, m=m, ns=ns: e.dma_start(
                out=att_s[m % 2], in_=at_sp[m * 128:(m + 1) * 128, ns]),
                writes=[("atts", m % 2)], dma=f"as{m % 2}")
            P.op(SY, lambda e, m=m, ns=ns: e.dma_start(
                out=x_s[m % 2], in_=x_fm[m * 128:(m + 1) * 128, ns]),
                writes=[("xs", m % 2)], dma=f"xs{m % 2}")
            if m + 1 < MT:
                c2_rb(n, m + 1)
            elif n + 1 < NTQ:
                c2_rb(n + 1, 0)
            P.op(VE, lambda e, m=m: e.reciprocal(rs_rep[m % 2],
                                                 rs_rep[m % 2]),
                 reads=[("rsr", m % 2)], writes=[("rsr", m % 2)])
            P.op(VE, lambda e, m=m: e.tensor_tensor(
                obs[m % 2], att_s[m % 2], rs_rep[m % 2], A.mult),
                reads=[("atts", m % 2), ("rsr", m % 2)], writes=[("ob", m % 2)])
            P.op(VE, lambda e, m=m, pp=pp: e.scalar_tensor_tensor(
                obs[m % 2], pp[:, (m % 4) * 512:(m % 4) * 512 + 512], 1.0,
                obs[m % 2], A.mult, A.add),
                reads=[pk, ("ob", m % 2)], writes=[("ob", m % 2)])
            P.op(VE, lambda e, m=m: e.tensor_add(obs[m % 2], obs[m % 2],
                                                 x_s[m % 2]),
                 reads=[("ob", m % 2), ("xs", m % 2)], writes=[("ob", m % 2)])
            P.op(GP, lambda e, m=m, ns=ns: e.dma_start(
                out=out_fm[m * 128:(m + 1) * 128, ns], in_=obs[m % 2]),
                reads=[("ob", m % 2)], dma="out")

    P.final_wait(SY)

    with contextlib.ExitStack() as st:
        sems = {}
        for i, name in enumerate(sorted(P.cnt)):
            sems[name] = st.enter_context(nc.semaphore(f"s{i}_{name[:8]}"))
        blk = st.enter_context(nc.Block())
        P.emit(blk, sems)

    return nc


def _layout_inputs(ins, S=2048):
    x = np.ascontiguousarray(ins["L0_attn_x"])
    z_q = np.ascontiguousarray(ins["L0_attn_z_q"])
    z_k = np.ascontiguousarray(ins["L0_attn_z_k"])
    w_v = np.ascontiguousarray(ins["L0_attn_w_v"])
    b_q = np.ascontiguousarray(ins["L0_attn_b_q"])
    b_k = np.ascontiguousarray(ins["L0_attn_b_k"])
    b_v = np.ascontiguousarray(ins["L0_attn_b_v"])
    w_g = np.ascontiguousarray(ins["L0_ffn_w_gate"])
    w_u = np.ascontiguousarray(ins["L0_ffn_w_up"])
    w_d = np.ascontiguousarray(ins["L0_ffn_w_down"])

    def tile_mtkt(z):
        return np.ascontiguousarray(
            z.reshape(KT, 128, MT, 128).transpose(2, 1, 0, 3).reshape(D, D))

    shared = {
        "zq_t": tile_mtkt(z_q),
        "zk_t": tile_mtkt(z_k),
        "wv_T": np.ascontiguousarray(w_v.T),
        "wg_T": np.ascontiguousarray(w_g.T),
        "wu_T": np.ascontiguousarray(w_u.T),
        "wd_T": np.ascontiguousarray(w_d.T),
        "bqk_pp": np.ascontiguousarray(
            np.concatenate([b_q.reshape(8, 128).T, b_k.reshape(8, 128).T],
                           axis=1)).astype(np.float32),
        "bv_row": np.ascontiguousarray(b_v.reshape(1, D)),
        "sel64": np.ascontiguousarray(
            (np.arange(16)[:, None] == (np.arange(D)[None, :] // 64))
            .astype(np.float32)),
        "eye": np.eye(128, dtype=np.float32),
        "consts": np.tile(np.array([[1e-5, -0.5, 0.5, 2.0, -2.0, 4.0, 0.125,
                                     -1.0]], dtype=np.float32), (128, 1)),
    }
    in_maps = []
    for b in range(x.shape[0]):
        m = dict(shared)
        m["x_fm"] = np.ascontiguousarray(x[b, :S, :].T)
        in_maps.append(m)
    return in_maps


_CACHE = {}


def kernel(**inputs):
    if "nc" not in _CACHE:
        _CACHE["nc"] = build()
    nc = _CACHE["nc"]
    in_maps = _layout_inputs(inputs)
    res = run_bass_kernel_spmd(nc, in_maps, core_ids=list(range(NCORE)))
    out = np.stack([np.asarray(r["out_fm"]).T for r in res.results], axis=0)
    return out.astype(np.float32)
